# version 34
# baseline (speedup 1.0000x reference)
import numpy as np
import ml_dtypes
from contextlib import ExitStack

import concourse.bass as bass
import concourse.mybir as mybir
from concourse.bass_utils import run_bass_kernel_spmd

F32 = mybir.dt.float32
BF16 = mybir.dt.bfloat16
FP8 = mybir.dt.float8e4
AX = mybir.AxisListType.X
AF = mybir.ActivationFunctionType
ALU = mybir.AluOpType

P = 128
S = 2048
D = 1024
FI = 4096
NSEQ = 2
NQT = S // P
ND = D // P
NI = FI // P
CRNE = 3.0 * 2.0**22
EPS = 1e-5
ISQD = 1.0 / 32.0
NCORES = 8


class Ctr:
    def __init__(self, sem):
        self.sem = sem
        self.n = 0

    def inc(self, instr, k=1):
        instr.then_inc(self.sem, k)
        self.n += k
        return self.n

    def reserve(self, k=1):
        self.n += k
        return self.n


class Sched:

    ORDER = ("sync", "gpsimd", "tensor", "vector", "scalar")

    def __init__(self):
        self.q = {k: [] for k in self.ORDER}

    def t(self, fn):
        self.q["tensor"].append(fn)

    def v(self, fn):
        self.q["vector"].append(fn)

    def s(self, fn):
        self.q["scalar"].append(fn)

    def g(self, fn):
        self.q["gpsimd"].append(fn)

    def sy(self, fn):
        self.q["sync"].append(fn)

    def flush(self, blk):
        for name in self.ORDER:
            fns = self.q[name]
            if not fns:
                continue

            def body(eng, fns=fns):
                for f in fns:
                    f(eng)
            getattr(blk, name)(body)
        self.q = {k: [] for k in self.ORDER}


def chain(addfn, ctr, ops, pre=()):
    marks = [ctr.reserve() for _ in ops]

    def f(e, ops=list(ops), marks=marks, sem=ctr.sem, pre=list(pre)):
        for ps, pv in pre:
            e.wait_ge(ps, pv)
        for i, opf in enumerate(ops):
            ins = opf(e)
            ins.then_inc(sem, 1)
            if i + 1 < len(ops):
                e.wait_ge(sem, marks[i])
    addfn(f)
    return marks[-1]


def build_program(debug=False):
    nc = bass.Bass("TRN2", target_bir_lowering=False)
    dbg = {}
    if debug:
        dbg["A"] = nc.dram_tensor("dbg_A", [P, ND * D], BF16, kind="ExternalOutput").ap()
        dbg["c2"] = nc.dram_tensor("dbg_c2", [P, ND], BF16, kind="ExternalOutput").ap()
        dbg["qT"] = nc.dram_tensor("dbg_qT", [P, ND * S], BF16, kind="ExternalOutput").ap()
        dbg["xT"] = nc.dram_tensor("dbg_xT", [P, ND * S], BF16, kind="ExternalOutput").ap()
        dbg["crow"] = nc.dram_tensor("dbg_crow", [1, D], BF16, kind="ExternalOutput").ap()
        dbg["vrow"] = nc.dram_tensor("dbg_vrow", [1, S], BF16, kind="ExternalOutput").ap()
        dbg["E0"] = nc.dram_tensor("dbg_E0", [P, S], F32, kind="ExternalOutput").ap()
        dbg["P0"] = nc.dram_tensor("dbg_P0", [P, S], BF16, kind="ExternalOutput").ap()
        dbg["st"] = nc.dram_tensor("dbg_st", [P, 40], F32, kind="ExternalOutput").ap()
        dbg["x2"] = nc.dram_tensor("dbg_x2", [S, D], F32, kind="ExternalOutput").ap()
        dbg["xq0"] = nc.dram_tensor("dbg_xq0", [P, D], BF16, kind="ExternalOutput").ap()
        dbg["hq0"] = nc.dram_tensor("dbg_hq0", [P, FI], BF16, kind="ExternalOutput").ap()
        dbg["h0"] = nc.dram_tensor("dbg_h0", [P, FI], F32, kind="ExternalOutput").ap()
        dbg["w8g"] = nc.dram_tensor("dbg_w8g", [P, FI], FP8, kind="ExternalOutput").ap()
        dbg["w8d"] = nc.dram_tensor("dbg_w8d", [P, D], FP8, kind="ExternalOutput").ap()

    x_d = nc.dram_tensor("x", [NSEQ, S, D], F32, kind="ExternalInput").ap()
    wq_d = nc.dram_tensor("Wq", [D, D], F32, kind="ExternalInput").ap()
    bq_d = nc.dram_tensor("bq", [D], F32, kind="ExternalInput").ap()
    wk_d = nc.dram_tensor("Wk", [D, D], F32, kind="ExternalInput").ap()
    bk_d = nc.dram_tensor("bk", [D], F32, kind="ExternalInput").ap()
    wg_d = nc.dram_tensor("Wg", [FI, D], F32, kind="ExternalInput").ap()
    wu_d = nc.dram_tensor("Wu", [FI, D], F32, kind="ExternalInput").ap()
    wd_d = nc.dram_tensor("Wd", [D, FI], F32, kind="ExternalInput").ap()
    y_d = nc.dram_tensor("y", [NSEQ, S, D], F32, kind="ExternalOutput").ap()

    xbf_d = nc.dram_tensor("xbf_scr", [S, D], BF16, kind="Internal").ap()
    x2_d = nc.dram_tensor("x2_scr", [S, D], F32, kind="Internal").ap()
    wgT8_d = nc.dram_tensor("wgT8_scr", [P, ND * FI], FP8, kind="Internal").ap()
    wuT8_d = nc.dram_tensor("wuT8_scr", [P, ND * FI], FP8, kind="Internal").ap()
    wdT8_d = nc.dram_tensor("wdT8_scr", [P, NI * D], FP8, kind="Internal").ap()
    a_scr = nc.dram_tensor("a_scr", [P, ND * D], BF16, kind="Internal").ap()
    wqbf_g = nc.dram_tensor("wqbf_g", [FI, D], BF16, kind="Internal").ap()
    wqbf_u = nc.dram_tensor("wqbf_u", [FI, D], BF16, kind="Internal").ap()
    wqbf_d = nc.dram_tensor("wqbf_d", [D, FI], BF16, kind="Internal").ap()

    ident_c = nc.inline_tensor(np.eye(P, dtype=ml_dtypes.bfloat16), "ident_c")
    onesr_b_c = nc.inline_tensor(np.ones((1, P), dtype=ml_dtypes.bfloat16), "onesr_b_c")
    onesc_b_c = nc.inline_tensor(np.ones((P, 1), dtype=ml_dtypes.bfloat16), "onesc_b_c")
    onesr_f_c = nc.inline_tensor(np.ones((1, P), dtype=np.float32), "onesr_f_c")
    onesc_f_c = nc.inline_tensor(np.ones((P, 1), dtype=np.float32), "onesc_f_c")
    neghalf_c = nc.inline_tensor(np.full((P, 1), -0.5, dtype=np.float32), "neghalf_c")
    crne_c = nc.inline_tensor(np.full((P, 1), CRNE, dtype=np.float32), "crne_c")

    with ExitStack() as ctx:
        e = ctx.enter_context

        ident = e(nc.sbuf_tensor("ident", [P, P], BF16))
        onesr_b = e(nc.sbuf_tensor("onesr_b", [1, P], BF16))
        onesc_b = e(nc.sbuf_tensor("onesc_b", [P, 1], BF16))
        onesr_f = e(nc.sbuf_tensor("onesr_f", [1, P], F32))
        onesc_f = e(nc.sbuf_tensor("onesc_f", [P, 1], F32))
        neghalf = e(nc.sbuf_tensor("neghalf", [P, 1], F32))
        crne_b = e(nc.sbuf_tensor("crne_b", [P, 1], F32))
        c2_sb = e(nc.sbuf_tensor("c2_sb", [P, ND], BF16))
        c0_sb = e(nc.sbuf_tensor("c0_sb", [1, 1], F32))
        bc_sb = e(nc.sbuf_tensor("bc_sb", [P, 2], F32))
        sc_sb = e(nc.sbuf_tensor("sc_sb", [1, 16], F32))
        sw_sb = e(nc.sbuf_tensor("sw_sb", [P, 1], F32))
        st_sb = e(nc.sbuf_tensor("st_sb", [P, 40], F32))
        stg = e(nc.sbuf_tensor("stg", [P, 8 * 1024], BF16))

        (NEGM, ZS, RCPZ, NMEAN, SUMSQ, VEPS, RSTD, NMR, AMAX, C1, SA,
         CGHU, HMAX, BCLIP, SEFF, CDOWN, T0, T1) = range(18)

        def stat(idx, par):
            c = idx * 2 + par
            return st_sb[:, c:c + 1]

        sLD = Ctr(e(nc.semaphore("sLD")))
        sGD = Ctr(e(nc.semaphore("sGD")))
        sST = Ctr(e(nc.semaphore("sST")))
        sPE = Ctr(e(nc.semaphore("sPE")))
        sV = Ctr(e(nc.semaphore("sV")))
        sSc = Ctr(e(nc.semaphore("sSc")))
        sG = Ctr(e(nc.semaphore("sG")))
        sL2 = Ctr(e(nc.semaphore("sL2")))

        xstage_done = [None, None]
        xT_done = [None, None]

        def emit_xstage(sch_, s):
            casts = [None] * NQT
            spills = [None] * NQT
            for c in range(NQT):
                casts[c] = sGD.reserve(16)
            for c in range(NQT):
                spills[c] = sST.reserve(16)

            def f_cast(g, s=s):
                for c in range(NQT):
                    if c >= 8:
                        g.wait_ge(sST.sem, spills[c - 8])
                    g.dma_start(out=stg[:, (c % 8) * 1024:(c % 8 + 1) * 1024],
                                in_=x_d[s, c * P:(c + 1) * P, :]).then_inc(sGD.sem, 16)
            sch_.g(f_cast)

            def f_spill(sy, s=s, pre=(xT_done[s - 1] if s >= 1 else None)):
                for c in range(NQT):
                    if c == 0 and pre is not None:
                        sy.wait_ge(sLD.sem, pre)
                    sy.wait_ge(sGD.sem, casts[c])
                    sy.dma_start(out=xbf_d[c * P:(c + 1) * P, :],
                                 in_=stg[:, (c % 8) * 1024:(c % 8 + 1) * 1024]).then_inc(sST.sem, 16)
            sch_.sy(f_spill)
            xstage_done[s] = spills[NQT - 1]

        with ExitStack() as p0:
            ep = p0.enter_context
            wqb = ep(nc.sbuf_tensor("wqb", [P, ND * D], BF16))
            wkb = ep(nc.sbuf_tensor("wkb", [P, ND * D], BF16))
            A_sb = ep(nc.sbuf_tensor("A_sb", [P, ND * D], BF16))
            bqb = ep(nc.sbuf_tensor("bqb", [P, ND], BF16))
            bkb = ep(nc.sbuf_tensor("bkb", [P, ND], BF16))
            aps = ep(nc.psum_tensor("aps", [P, 1024], F32))
            cps = ep(nc.psum_tensor("cps", [P, 1], F32))
            c0ps = ep(nc.psum_tensor("c0ps", [1, 1], F32))
            blk = ep(nc.Block("p0"))
            sch0 = Sched()
            sch0.flush(blk)

            @blk.sync
            def _(sy):
                for dst, src in ((ident, ident_c), (onesr_b, onesr_b_c),
                                 (onesc_b, onesc_b_c), (onesr_f, onesr_f_c),
                                 (onesc_f, onesc_f_c), (neghalf, neghalf_c),
                                 (crne_b, crne_c)):
                    sy.dma_start(out=dst[:], in_=src[:, :]).then_inc(sLD.sem, 16)
                    sLD.n += 16
                sy.wait_ge(sLD.sem, sLD.n)

            @blk.gpsimd
            def _(g):
                g.dma_start(out=wqb[:].rearrange("p (c d) -> p c d", d=D),
                            in_=wq_d.rearrange("(c p) d -> p c d", p=P)).then_inc(sGD.sem, 16)
                sGD.n += 16
                g.dma_start(out=wkb[:].rearrange("p (c d) -> p c d", d=D),
                            in_=wk_d.rearrange("(c p) d -> p c d", p=P)).then_inc(sGD.sem, 16)
                sGD.n += 16
                g.dma_start(out=bqb[:], in_=bq_d.rearrange("(c p) -> p c", p=P),
                            allow_slow_non_contiguous=True).then_inc(sGD.sem, 16)
                sGD.n += 16
                g.dma_start(out=bkb[:], in_=bk_d.rearrange("(c p) -> p c", p=P),
                            allow_slow_non_contiguous=True).then_inc(sGD.sem, 16)
                sGD.n += 16
            wk_loaded = sGD.n

            a_pe = []
            a_ev = []
            for k in range(2 * ND):
                dc, eh = k // 2, k % 2

                @blk.tensor
                def _(t, dc=dc, eh=eh, k=k):
                    if k == 0:
                        t.wait_ge(sGD.sem, wk_loaded)
                    if k >= 2:
                        t.wait_ge(sV.sem, a_ev[k - 2])
                    for fc in range(ND):
                        ins = t.matmul(aps[:, eh * 512:(eh + 1) * 512],
                                       wqb[:, fc * D + dc * P: fc * D + (dc + 1) * P],
                                       wkb[:, fc * D + eh * 512: fc * D + (eh + 1) * 512],
                                       start=(fc == 0), stop=(fc == ND - 1))
                    sPE.inc(ins)
                a_pe.append(sPE.n)

                @blk.vector
                def _(v, dc=dc, eh=eh, k=k):
                    v.wait_ge(sPE.sem, a_pe[k])
                    sV.inc(v.tensor_scalar_mul(
                        A_sb[:, dc * D + eh * 512: dc * D + (eh + 1) * 512],
                        aps[:, eh * 512:(eh + 1) * 512], ISQD))
                a_ev.append(sV.n)

            c2_pe = []
            c2_ev = []
            for dc in range(ND):
                @blk.tensor
                def _(t, dc=dc):
                    if dc >= 1:
                        t.wait_ge(sV.sem, c2_ev[dc - 1])
                    for fc in range(ND):
                        ins = t.matmul(cps[:], wkb[:, fc * D + dc * P: fc * D + (dc + 1) * P],
                                       bqb[:, fc:fc + 1], start=(fc == 0), stop=(fc == ND - 1))
                    sPE.inc(ins)
                c2_pe.append(sPE.n)

                @blk.vector
                def _(v, dc=dc):
                    v.wait_ge(sPE.sem, c2_pe[dc])
                    sV.inc(v.tensor_scalar_mul(c2_sb[:, dc:dc + 1], cps[:], ISQD))
                c2_ev.append(sV.n)

            @blk.tensor
            def _(t):
                for fc in range(ND):
                    ins = t.matmul(c0ps[:], bqb[:, fc:fc + 1], bkb[:, fc:fc + 1],
                                   start=(fc == 0), stop=(fc == ND - 1))
                sPE.inc(ins)
            c0_pe = sPE.n

            @blk.vector
            def _(v):
                v.wait_ge(sPE.sem, c0_pe)
                sV.inc(v.tensor_scalar_mul(c0_sb[:], c0ps[:], ISQD))
            p0_v_done = sV.n

            @blk.gpsimd
            def _(g):
                g.wait_ge(sV.sem, p0_v_done)
                g.dma_start(out=a_scr, in_=A_sb[:]).then_inc(sGD.sem, 16)
                sGD.n += 16
                if debug:
                    g.dma_start(out=dbg["A"], in_=A_sb[:]).then_inc(sGD.sem, 16)
                    sGD.n += 16
                    g.dma_start(out=dbg["c2"], in_=c2_sb[:]).then_inc(sGD.sem, 16)
                    sGD.n += 16
            a_spilled = sGD.n

        wp_spill_done = [None]
        wp_cast_done = [None]

        def weight_prep_all():
            fdim = 32 * 1024
            nch = 16
            wspecs = [(wg_d, wqbf_g, FI, D), (wu_d, wqbf_u, FI, D), (wd_d, wqbf_d, D, FI)]
            with ExitStack() as px:
                ep2 = px.enter_context
                wst = ep2(nc.sbuf_tensor("wst", [P, fdim], F32))
                tqp = ep2(nc.sbuf_tensor("tqp", [P, 3 * 2048], F32))
                uqp = ep2(nc.sbuf_tensor("uqp", [P, 2048], F32))
                wqn = ep2(nc.sbuf_tensor("wqn", [P, 4 * 2048], BF16))
                junkw = ep2(nc.sbuf_tensor("junkw", [P, 2048], F32))
                wsm = ep2(nc.sbuf_tensor("wsm", [P, 18], F32))
                tps = ep2(nc.psum_tensor("tps", [P, 2], F32))
                blk = ep2(nc.Block("wp0"))
                sch = Sched()

                s_mk = [[None] * nch for _ in range(3)]
                v_mk = [[None] * nch for _ in range(3)]
                sp_mk = [[None] * nch for _ in range(3)]

                xs_cast = [None] * NQT
                xs_spill = [None] * NQT
                xs_next = [0]

                def emit_xjob():
                    c = xs_next[0]
                    if c >= NQT:
                        return
                    xs_next[0] = c + 1
                    xs_cast[c] = sGD.reserve(16)

                    def f_xc(g, c=c, wslot=(xs_spill[c - 8] if c >= 8 else None)):
                        if wslot is not None:
                            g.wait_ge(sST.sem, wslot)
                        g.dma_start(out=stg[:, (c % 8) * 1024:(c % 8 + 1) * 1024],
                                    in_=x_d[0, c * P:(c + 1) * P, :]).then_inc(sGD.sem, 16)
                    sch.g(f_xc)
                    if c >= 1:
                        xs_spill[c - 1] = sST.reserve(16)

                        def f_xs(g, c=c - 1, w=xs_cast[c - 1]):
                            g.wait_ge(sGD.sem, w)
                            g.dma_start(out=xbf_d[c * P:(c + 1) * P, :],
                                        in_=stg[:, (c % 8) * 1024:(c % 8 + 1) * 1024]
                                        ).then_inc(sST.sem, 16)
                        sch.g(f_xs)

                def emit_xtail():
                    while xs_next[0] < NQT:
                        emit_xjob()
                    xs_spill[NQT - 1] = sST.reserve(16)

                    def f_xs(g, w=xs_cast[NQT - 1]):
                        g.wait_ge(sGD.sem, w)
                        g.dma_start(out=xbf_d[(NQT - 1) * P:NQT * P, :],
                                    in_=stg[:, ((NQT - 1) % 8) * 1024:((NQT - 1) % 8 + 1) * 1024]
                                    ).then_inc(sST.sem, 16)
                    sch.g(f_xs)
                    xstage_done[0] = xs_spill[NQT - 1]

                def spill_args(swidx, sk):
                    wq_dram = wspecs[swidx][1]
                    cols = wspecs[swidx][3]
                    par = sk % 4
                    if cols == 1024:
                        outap = wq_dram[2 * sk * P:(2 * sk + 2) * P, :].rearrange(
                            "(t p) d -> p t d", p=P)
                        inap = wqn[:, par * 2048:(par + 1) * 2048].rearrange(
                            "p (t f) -> p t f", f=1024)
                    else:
                        c_, hh = sk // 2, sk % 2
                        outap = wq_dram[c_ * P:(c_ + 1) * P, hh * 2048:(hh + 1) * 2048]
                        inap = wqn[:, par * 2048:(par + 1) * 2048]
                    return outap, inap

                def slab_args(swidx, g4):
                    w_dram = wspecs[swidx][0]
                    rows, cols = wspecs[swidx][2], wspecs[swidx][3]
                    nt4 = (rows // P) // 4
                    outap = wst[:, g4 * 8192:(g4 + 1) * 8192].rearrange(
                        "p (c d) -> p c d", d=cols)
                    inap = w_dram[g4 * nt4 * P:(g4 + 1) * nt4 * P, :].rearrange(
                        "(c p) d -> p c d", p=P)
                    return outap, inap

                for _ in range(8):
                    emit_xjob()

                for widx, (w_dram, wq_dram, rows, cols) in enumerate(wspecs):
                    ntile = rows // P

                    w_ld4 = [(sLD, sLD.reserve(16)) if g4 % 2 == 0 else (sL2, sL2.reserve(16))
                             for g4 in range(4)]

                    def f_wld(sy, widx=widx,
                              marks=(s_mk[widx - 1] if widx >= 1 else None)):
                        for g4 in (0, 2):
                            if marks is not None:
                                sy.wait_ge(sSc.sem, marks[4 * g4 + 3])
                            outap, inap = slab_args(widx, g4)
                            sy.dma_start(out=outap, in_=inap).then_inc(sLD.sem, 16)
                    sch.sy(f_wld)

                    if widx == 0:
                        def f_wlds0(s):
                            for g4 in (1, 3):
                                outap, inap = slab_args(0, g4)
                                s.dma_start(out=outap, in_=inap).then_inc(sL2.sem, 16)
                        sch.s(f_wlds0)

                    s_abs = sSc.reserve(8)

                    def f_abs(s, w_ld4=w_ld4):
                        for k in range(0, 16, 2):
                            if k % 4 == 0:
                                ctr, mk = w_ld4[k // 4]
                                s.wait_ge(ctr.sem, mk)
                            s.activation(junkw[:], wst[:, k * 2048:(k + 1) * 2048],
                                         AF.Abs, accum_out=wsm[:, k:k + 1]).then_inc(sSc.sem, 1)
                    sch.s(f_abs)

                    v_abs0 = sV.reserve(8)

                    def f_vabs0(v, w_ld4=w_ld4):
                        for k in range(1, 16, 2):
                            if k % 4 == 1:
                                ctr, mk = w_ld4[k // 4]
                                v.wait_ge(ctr.sem, mk)
                            v.reduce_sum(wsm[:, k:k + 1], wst[:, k * 2048:(k + 1) * 2048],
                                         axis=AX, apply_absolute_value=True).then_inc(sV.sem, 1)
                    sch.v(f_vabs0)

                    v_abs = sV.reserve()

                    def f_vabs(v, w=s_abs, wv=v_abs0):
                        v.wait_ge(sSc.sem, w)
                        v.wait_ge(sV.sem, wv)
                        v.reduce_sum(wsm[:, 16:17], wsm[:, 0:16], axis=AX).then_inc(sV.sem, 1)
                    sch.v(f_vabs)

                    pe_tot = sPE.reserve()

                    def f_tot(t, w=v_abs):
                        t.wait_ge(sV.sem, w)
                        t.matmul(tps[:1, 0:1], onesc_f[:], wsm[:, 16:17],
                                 start=True, stop=True).then_inc(sPE.sem, 1)
                    sch.t(f_tot)

                    v_sc = chain(sch.v, sV, [
                        (lambda v, rows=rows, cols=cols: v.tensor_scalar(
                            sc_sb[:, 0:1], tps[:1, 0:1], 1.0 / float(rows * cols), EPS,
                            op0=ALU.mult, op1=ALU.max)),
                        (lambda v: v.reciprocal(sc_sb[:, 1:2], sc_sb[:, 0:1])),
                    ], pre=[(sPE.sem, pe_tot)])

                    pe_b = sPE.reserve()

                    def f_bc(t, w=v_sc):
                        t.wait_ge(sV.sem, w)
                        t.matmul(tps[:, 1:2], onesr_f[:], sc_sb[:, 1:2],
                                 start=True, stop=True).then_inc(sPE.sem, 1)
                    sch.t(f_bc)

                    v_swb = sV.reserve()

                    def f_swb(v, w=pe_b, widx=widx,
                              wprev=(s_mk[widx - 1][nch - 1] if widx >= 1 else None)):
                        v.wait_ge(sPE.sem, w)
                        if wprev is not None:
                            v.wait_ge(sSc.sem, wprev)
                        v.tensor_copy(sw_sb[:], tps[:, 1:2]).then_inc(sV.sem, 1)
                    sch.v(f_swb)

                    for k in range(nch):
                        par = k % 3

                        s_mk[widx][k] = sSc.reserve()

                        def f_qa(s, widx=widx, k=k, par=par, w0=v_swb,
                                 w2=(v_mk[widx][k - 3] if k >= 3 else
                                     (v_mk[widx - 1][nch - 3 + k] if widx >= 1 else None))):
                            if k == 0:
                                s.wait_ge(sV.sem, w0)
                            if w2 is not None:
                                s.wait_ge(sV.sem, w2)
                            s.activation(tqp[:, par * 2048:(par + 1) * 2048],
                                         wst[:, k * 2048:(k + 1) * 2048],
                                         AF.Identity, bias=crne_b[:],
                                         scale=sw_sb[:]).then_inc(sSc.sem, 1)
                            if k in (7, 15) and widx + 1 < 3:
                                outap, inap = slab_args(widx + 1, 1 if k == 7 else 3)
                                s.dma_start(out=outap, in_=inap).then_inc(sL2.sem, 16)
                        sch.s(f_qa)

                        m_u = sV.reserve()
                        v_mk[widx][k] = sV.reserve()

                        sp_mk[widx][k] = sST.reserve(16)

                        parw = k % 4

                        def f_qb(v, widx=widx, k=k, par=par, parw=parw, w=s_mk[widx][k],
                                 m1=m_u,
                                 wsp=(sp_mk[widx][k - 4] if k >= 4 else
                                      (sp_mk[widx - 1][nch - 4 + k] if widx >= 1 else None))):
                            v.wait_ge(sSc.sem, w)
                            if wsp is not None:
                                v.wait_ge(sST.sem, wsp)
                            v.tensor_scalar(uqp[:], tqp[:, par * 2048:(par + 1) * 2048],
                                            CRNE - 1.0, -CRNE,
                                            op0=ALU.max, op1=ALU.add).then_inc(sV.sem, 1)
                            v.wait_ge(sV.sem, m1)
                            v.tensor_scalar_min(wqn[:, parw * 2048:(parw + 1) * 2048],
                                                uqp[:], 1.0).then_inc(sV.sem, 1)
                        sch.v(f_qb)

                        def f_spg(g, widx=widx, k=k, w=v_mk[widx][k]):
                            g.wait_ge(sV.sem, w)
                            outap, inap = spill_args(widx, k)
                            g.dma_start(out=outap, in_=inap).then_inc(sST.sem, 16)
                        sch.g(f_spg)
                        if (widx * nch + k) % 6 == 5:
                            emit_xjob()

                    def f_save(v, widx=widx):
                        v.tensor_copy(sc_sb[:, 2 + widx:3 + widx],
                                      sc_sb[:, 0:1]).then_inc(sV.sem, 1)
                    sV.reserve()
                    sch.v(f_save)

                emit_xtail()
                wp_spill_done[0] = sST.n
                sch.flush(blk)
            return

        weight_prep_all()

        with ExitStack() as pf:
            epf = pf.enter_context
            fps = epf(nc.psum_tensor("fps", [P, 2], F32))
            blk = epf(nc.Block("pfold"))

            @blk.vector
            def _(v):
                sV.inc(v.tensor_tensor(sc_sb[:, 5:6], sc_sb[:, 2:3], sc_sb[:, 2:3], op=ALU.mult))
                v.wait_ge(sV.sem, sV.n)
                sV.inc(v.tensor_tensor(sc_sb[:, 5:6], sc_sb[:, 5:6], sc_sb[:, 3:4], op=ALU.mult))
                v.wait_ge(sV.sem, sV.n)
                sV.inc(v.tensor_scalar_mul(sc_sb[:, 6:7], sc_sb[:, 4:5], 1.0 / 127.0))
            v_f = sV.n

            @blk.tensor
            def _(t):
                t.wait_ge(sV.sem, v_f)
                t.matmul(fps[:, 0:1], onesr_f[:], sc_sb[:, 5:6], start=True, stop=True)
                ins = t.matmul(fps[:, 1:2], onesr_f[:], sc_sb[:, 6:7], start=True, stop=True)
                sPE.inc(ins)
            pe_f = sPE.n

            @blk.vector
            def _(v):
                v.wait_ge(sPE.sem, pe_f)
                sV.inc(v.tensor_copy(bc_sb[:, 0:2], fps[:, 0:2]))

        for s in range(NSEQ):
            x2_spill = [None] * NQT

            with ExitStack() as ax:
                ea = ax.enter_context
                xtok = ea(nc.sbuf_tensor(f"xtok{s}", [P, NQT * D], BF16))
                A_loc = ea(nc.sbuf_tensor(f"A_loc{s}", [P, ND * D], BF16))
                xT = ea(nc.sbuf_tensor(f"xT{s}", [P, ND * S], BF16, is_transpose_dest=True))
                qT = ea(nc.sbuf_tensor(f"qT{s}", [P, ND * S], BF16))
                Eb = ea(nc.sbuf_tensor(f"Eb{s}", [P, S], F32))
                Pb = ea(nc.sbuf_tensor(f"Pb{s}", [P, S], BF16))
                PT = ea(nc.sbuf_tensor(f"PT{s}", [P, S], BF16))
                xr = ea(nc.sbuf_tensor(f"xr{s}", [P, 2 * D], F32))
                ot = ea(nc.sbuf_tensor(f"ot{s}", [P, 2 * D], F32))
                junk = ea(nc.sbuf_tensor(f"junk{s}", [P, D], F32))
                x2o = ea(nc.sbuf_tensor(f"x2o{s}", [P, 2 * D], F32))
                vrow = ea(nc.sbuf_tensor(f"vrow{s}", [1, S], BF16))
                wtt = ea(nc.sbuf_tensor(f"wtt{s}", [P, 3 * FI], BF16,
                                        is_transpose_dest=True)) if s == 0 else None
                crow = ea(nc.sbuf_tensor(f"crow{s}", [1, D], BF16))
                spsum = ea(nc.psum_tensor(f"spsum{s}", [P, S], F32))
                accp = ea(nc.psum_tensor(f"accp{s}", [P, D], F32))
                ptp = ea(nc.psum_tensor(f"ptp{s}", [P, S], BF16))
                blk = ea(nc.Block(f"attn{s}"))
                sch = Sched()

                xtok_ld = sGD.reserve(16)

                def f_xtok(g, s=s):
                    g.dma_start(out=xtok[:].rearrange("p (t d) -> p t d", d=D),
                                in_=x_d[s].rearrange("(t p) d -> p t d", p=P)).then_inc(sGD.sem, 16)
                sch.g(f_xtok)

                a_loaded = sLD.reserve(16)

                def f_aload(sy, w=a_spilled):
                    sy.wait_ge(sGD.sem, w)
                    sy.dma_start(out=A_loc[:], in_=a_scr).then_inc(sLD.sem, 16)
                sch.sy(f_aload)

                xT_all = sLD.reserve(16 * ND)
                xT_done[s] = xT_all

                def f_xT(sy, w=xstage_done[s]):
                    sy.wait_ge(sST.sem, w)
                    for dc in range(ND):
                        sy.dma_start(out=xT[:, dc * S:(dc + 1) * S],
                                     in_=xbf_d[:, dc * P:(dc + 1) * P],
                                     transpose=True).then_inc(sLD.sem, 16)
                sch.sy(f_xT)

                mk_xr = {}
                for i0 in range(2):
                    mk_xr[i0] = sGD.reserve(16)

                def f_xr01(g, s=s):
                    for i0 in range(2):
                        g.dma_start(out=xr[:, i0 * D:(i0 + 1) * D],
                                    in_=x_d[s, i0 * P:(i0 + 1) * P, :]).then_inc(sGD.sem, 16)
                sch.g(f_xr01)

                vr_pe = [sPE.reserve() for _ in range(6)]

                def f_vrow_pe(t, wxt=xT_all, wxtok=xtok_ld):
                    t.wait_ge(sLD.sem, wxt)
                    for vc in range(4):
                        for dc in range(ND):
                            ins = t.matmul(spsum[:1, vc * 512:(vc + 1) * 512],
                                           c2_sb[:, dc:dc + 1],
                                           xT[:, dc * S + vc * 512: dc * S + (vc + 1) * 512],
                                           start=(dc == 0), stop=(dc == ND - 1))
                        ins.then_inc(sPE.sem, 1)
                    for b in range(2):
                        for tc in range(NQT):
                            ins = t.matmul(accp[:1, b * 512:(b + 1) * 512],
                                           onesc_b[:],
                                           xtok[:, tc * D + b * 512: tc * D + (b + 1) * 512],
                                           start=(tc == 0), stop=(tc == NQT - 1))
                        ins.then_inc(sPE.sem, 1)
                sch.t(f_vrow_pe)

                vrow_ev = sV.reserve(6)

                def f_vrow_ev(v, marks=vr_pe):
                    for vc in range(4):
                        v.wait_ge(sPE.sem, marks[vc])
                        v.tensor_scalar(vrow[:, vc * 512:(vc + 1) * 512],
                                        spsum[:1, vc * 512:(vc + 1) * 512],
                                        c0_sb[:, :], None, op0=ALU.add).then_inc(sV.sem, 1)
                    v.wait_ge(sPE.sem, marks[5])
                    for b in range(2):
                        v.tensor_scalar_mul(crow[:, b * 512:(b + 1) * 512],
                                            accp[:1, b * 512:(b + 1) * 512],
                                            -2.0 / S).then_inc(sV.sem, 1)
                sch.v(f_vrow_ev)

                qg_pe = [None] * 16
                qg_ev = [None] * 16
                for it in range(16):
                    ec, sh, half = it // 2, it % 2, it % 2
                    qg_pe[it] = sPE.reserve()

                    def f_qg(t, ec=ec, sh=sh, half=half, it=it,
                             wv=vrow_ev, wa=a_loaded,
                             we=(qg_ev[it - 2] if it >= 2 else None)):
                        if it == 0:
                            t.wait_ge(sV.sem, wv)
                            t.wait_ge(sLD.sem, wa)
                        if we is not None:
                            t.wait_ge(sV.sem, we)
                        for dc in range(ND):
                            for s2 in range(2):
                                ins = t.matmul(
                                    spsum[:, half * D + s2 * 512: half * D + (s2 + 1) * 512],
                                    A_loc[:, dc * D + ec * P: dc * D + (ec + 1) * P],
                                    xT[:, dc * S + sh * D + s2 * 512: dc * S + sh * D + (s2 + 1) * 512],
                                    start=(dc == 0), stop=(dc == ND - 1))
                        ins.then_inc(sPE.sem, 1)
                    sch.t(f_qg)

                    qg_ev[it] = sV.reserve()

                    def f_qev(v, ec=ec, sh=sh, half=half, w=qg_pe[it]):
                        v.wait_ge(sPE.sem, w)
                        v.tensor_copy(qT[:, ec * S + sh * D: ec * S + (sh + 1) * D],
                                      spsum[:, half * D:(half + 1) * D]).then_inc(sV.sem, 1)
                    sch.v(f_qev)
                q_all = qg_ev[15]

                if debug and s == 0:
                    def f_dbgq(g, w=q_all):
                        g.wait_ge(sV.sem, w)
                        g.dma_start(out=dbg["qT"], in_=qT[:]).then_inc(sGD.sem, 16)
                        g.dma_start(out=dbg["xT"], in_=xT[:]).then_inc(sGD.sem, 16)
                        g.dma_start(out=dbg["vrow"], in_=vrow[:]).then_inc(sGD.sem, 16)
                        g.dma_start(out=dbg["crow"], in_=crow[:]).then_inc(sGD.sem, 16)
                    sch.g(f_dbgq)
                    sGD.n += 64

                if s == 0:
                    wjobs = ([("g", dc) for dc in range(ND)]
                             + [("u", dc) for dc in range(ND)]
                             + [("d", ic) for ic in range(NI)])
                    wj_tl = [None] * len(wjobs)
                    wj_cs = [None] * len(wjobs)

                    def emit_wtl(j):
                        wj_tl[j] = sLD.reserve(16)

                        assert j < 3 or wj_cs[j - 3] is not None, f"wtl {j} guard missing"

                        def f_tl(sy, j=j, wcs=(wj_cs[j - 3] if j >= 3 else None),
                                 wsp=wp_spill_done[0]):
                            kind, idx = wjobs[j]
                            if j == 0:
                                sy.wait_ge(sST.sem, wsp)
                            if wcs is not None:
                                sy.wait_ge(sGD.sem, wcs)
                            if kind == "g":
                                inap = wqbf_g[:, idx * P:(idx + 1) * P]
                                wid = FI
                            elif kind == "u":
                                inap = wqbf_u[:, idx * P:(idx + 1) * P]
                                wid = FI
                            else:
                                inap = wqbf_d[:, idx * P:(idx + 1) * P]
                                wid = D
                            sy.dma_start(out=wtt[:, (j % 3) * FI:(j % 3) * FI + wid],
                                         in_=inap, transpose=True).then_inc(sLD.sem, 16)
                        sch.sy(f_tl)

                    def emit_wcs(j):
                        wj_cs[j] = sGD.reserve(16)

                        def f_cs(g, j=j, wtl=wj_tl[j]):
                            kind, idx = wjobs[j]
                            g.wait_ge(sLD.sem, wtl)
                            if kind == "g":
                                out = wgT8_d[:, idx * FI:(idx + 1) * FI]
                                wid = FI
                            elif kind == "u":
                                out = wuT8_d[:, idx * FI:(idx + 1) * FI]
                                wid = FI
                            else:
                                out = wdT8_d[:, idx * D:(idx + 1) * D]
                                wid = D
                            g.dma_start(out=out,
                                        in_=wtt[:, (j % 3) * FI:(j % 3) * FI + wid]).then_inc(sGD.sem, 16)
                        sch.g(f_cs)

                mk_scores = [None] * NQT
                mk_rowmax = [None] * NQT
                mk_exp = [None] * NQT
                mk_recip = [None] * NQT
                mk_pcast = [None] * NQT
                mk_pt = [None] * NQT
                mk_ptev = [None] * NQT
                mk_pax = [None] * NQT
                mk_epi = [None] * NQT
                mk_epi_a = [None] * NQT
                mk_nmean = [None] * NQT
                mk_lnsq = [None] * NQT
                mk_veps = [None] * NQT
                mk_nmr = [None] * NQT
                mk_apply = [None] * NQT

                for i in range(NQT + 2):
                    j = i - 1
                    p_ = i % 2
                    pj = j % 2

                    if s == 0:
                        for jb in range(3 * (i - 1), 3 * i):
                            if 0 <= jb < len(wjobs):
                                emit_wcs(jb)
                        for jb in range(3 * i, 3 * (i + 1)):
                            if jb < len(wjobs):
                                emit_wtl(jb)

                    if 0 <= j < NQT:
                        mk_pt[j] = sPE.reserve()

                        def f_pt(t, j=j, wp=mk_pcast[j],
                                 we=(mk_ptev[j - 1] if j >= 1 else None)):
                            t.wait_ge(sSc.sem, wp)
                            if we is not None:
                                t.wait_ge(sV.sem, we)
                            for tc in range(NQT):
                                ins = t.transpose(ptp[:, tc * P:(tc + 1) * P],
                                                  Pb[:, tc * P:(tc + 1) * P], ident[:])
                            ins.then_inc(sPE.sem, 1)
                        sch.t(f_pt)

                    if i < NQT:
                        mk_scores[i] = sPE.reserve()

                        def f_sc(t, i=i, we=(mk_exp[i - 1] if i >= 1 else None), wq=q_all):
                            if we is not None:
                                t.wait_ge(sSc.sem, we)
                            else:
                                t.wait_ge(sV.sem, wq)
                            for ec in range(ND):
                                for b in range(4):
                                    t.matmul(spsum[:, b * 512:(b + 1) * 512],
                                             qT[:, ec * S + i * P: ec * S + (i + 1) * P],
                                             xT[:, ec * S + b * 512: ec * S + (b + 1) * 512],
                                             start=(ec == 0), stop=False)
                            for b in range(4):
                                ins = t.matmul(spsum[:, b * 512:(b + 1) * 512],
                                               onesr_b[:], vrow[:, b * 512:(b + 1) * 512],
                                               start=False, stop=True)
                            ins.then_inc(sPE.sem, 1)
                        sch.t(f_sc)

                    if 0 <= j < NQT:
                        mk_ptev[j] = sV.reserve()

                        def f_ptev(v, j=j, w=mk_pt[j],
                                   wx=(mk_pax[j - 1] if j >= 1 else None)):
                            v.wait_ge(sPE.sem, w)
                            if wx is not None:
                                v.wait_ge(sPE.sem, wx)
                            v.tensor_copy(PT[:], ptp[:]).then_inc(sV.sem, 1)
                        sch.v(f_ptev)

                    if i < NQT:
                        mk_rowmax[i] = sV.reserve()

                        def f_rm(v, i=i, p_=p_, w=mk_scores[i]):
                            v.wait_ge(sPE.sem, w)
                            v.tensor_reduce(stat(NEGM, p_), spsum[:], axis=AX,
                                            op=ALU.max, negate=True).then_inc(sV.sem, 1)
                        sch.v(f_rm)

                    if i < NQT:
                        mk_exp[i] = sSc.reserve()

                        def f_exp(sce, i=i, p_=p_, w=mk_rowmax[i]):
                            sce.wait_ge(sV.sem, w)
                            sce.activation(Eb[:], spsum[:], AF.Exp,
                                           bias=stat(NEGM, p_), scale=1.0,
                                           accum_out=stat(ZS, p_)).then_inc(sSc.sem, 1)
                        sch.s(f_exp)

                        mk_recip[i] = sV.reserve()

                        def f_rcp(v, i=i, p_=p_, w=mk_exp[i]):
                            v.wait_ge(sSc.sem, w)
                            v.reciprocal(stat(RCPZ, p_), stat(ZS, p_)).then_inc(sV.sem, 1)
                        sch.v(f_rcp)

                        mk_pcast[i] = sSc.reserve()

                        def f_pc(sce, i=i, p_=p_, w=mk_recip[i],
                                 wp=(mk_pt[i - 1] if i >= 1 else None)):
                            sce.wait_ge(sV.sem, w)
                            if wp is not None:
                                sce.wait_ge(sPE.sem, wp)
                            sce.activation(Pb[:], Eb[:], AF.Identity,
                                           scale=stat(RCPZ, p_)).then_inc(sSc.sem, 1)
                        sch.s(f_pc)

                        if debug and s == 0 and i == 0:
                            def f_dbgp(g, w=mk_pcast[0]):
                                g.wait_ge(sSc.sem, w)
                                g.dma_start(out=dbg["E0"], in_=Eb[:]).then_inc(sGD.sem, 16)
                                g.dma_start(out=dbg["P0"], in_=Pb[:]).then_inc(sGD.sem, 16)
                                g.dma_start(out=dbg["st"], in_=st_sb[:]).then_inc(sGD.sem, 16)
                            sch.g(f_dbgp)
                            sGD.n += 48

                    if 0 <= j < NQT:
                        mk_pax[j] = sPE.reserve()

                        def f_pax(t, j=j, w=mk_ptev[j],
                                  wea=(mk_epi_a[j - 1] if j >= 1 else None),
                                  web=(mk_epi[j - 1] if j >= 1 else None)):
                            t.wait_ge(sV.sem, w)
                            for b in range(2):
                                we = wea if b == 0 else web
                                if we is not None:
                                    t.wait_ge(sV.sem, we)
                                t.matmul(accp[:, b * 512:(b + 1) * 512],
                                         onesr_b[:], crow[:, b * 512:(b + 1) * 512],
                                         start=True, stop=False)
                                for tc in range(NQT):
                                    ins = t.matmul(accp[:, b * 512:(b + 1) * 512],
                                                   PT[:, tc * P:(tc + 1) * P],
                                                   xtok[:, tc * D + b * 512: tc * D + (b + 1) * 512],
                                                   start=False, stop=(tc == NQT - 1))
                            ins.then_inc(sPE.sem, 1)
                        sch.t(f_pax)

                        mk_epi_a[j] = sV.reserve()
                        mk_epi[j] = sV.reserve()
                        m_ms = sV.reserve()
                        mk_nmean[j] = sV.reserve()

                        def f_epi(v, j=j, pj=pj, w=mk_pax[j], wxr=mk_xr[j],
                                  wa=(mk_apply[j - 2] if j >= 2 else None),
                                  wn=(mk_nmr[j - 2] if j >= 2 else None),
                                  m1=mk_epi[j], m2=m_ms):
                            v.wait_ge(sPE.sem, w)
                            v.wait_ge(sGD.sem, wxr)
                            if wa is not None:
                                v.wait_ge(sSc.sem, wa)
                            v.scalar_tensor_tensor(ot[:, pj * D:pj * D + 512],
                                                   xr[:, pj * D:pj * D + 512],
                                                   2.0, accp[:, 0:512],
                                                   op0=ALU.mult, op1=ALU.add).then_inc(sV.sem, 1)
                            v.scalar_tensor_tensor(ot[:, pj * D + 512:(pj + 1) * D],
                                                   xr[:, pj * D + 512:(pj + 1) * D],
                                                   2.0, accp[:, 512:1024],
                                                   op0=ALU.mult, op1=ALU.add).then_inc(sV.sem, 1)
                            if wn is not None:
                                v.wait_ge(sG.sem, wn)
                            v.wait_ge(sV.sem, m1)
                            v.tensor_reduce(stat(SUMSQ, pj), ot[:, pj * D:(pj + 1) * D],
                                            axis=AX, op=ALU.add).then_inc(sV.sem, 1)
                            v.wait_ge(sV.sem, m2)
                            v.tensor_scalar_mul(stat(NMEAN, pj), stat(SUMSQ, pj),
                                                -1.0 / D).then_inc(sV.sem, 1)
                        sch.v(f_epi)

                        if i + 1 < NQT:
                            mk_xr[i + 1] = sGD.reserve(16)

                            def f_xrp(g, s=s, i=i, w=mk_epi[j]):
                                g.wait_ge(sV.sem, w)
                                g.dma_start(out=xr[:, ((i + 1) % 2) * D:((i + 1) % 2 + 1) * D],
                                            in_=x_d[s, (i + 1) * P:(i + 2) * P, :]).then_inc(sGD.sem, 16)
                            sch.g(f_xrp)

                    jj = i - 2
                    if 0 <= jj < NQT:
                        pjj = jj % 2

                        mk_lnsq[jj] = sSc.reserve()

                        def f_lnsq(sce, jj=jj, pjj=pjj, w=mk_nmean[jj]):
                            sce.wait_ge(sV.sem, w)
                            sce.activation(junk[:], ot[:, pjj * D:(pjj + 1) * D],
                                           AF.Square, bias=stat(NMEAN, pjj), scale=1.0,
                                           accum_out=stat(SUMSQ, pjj)).then_inc(sSc.sem, 1)
                        sch.s(f_lnsq)

                        m_veps = sG.reserve()
                        m_rstd = sG.reserve()
                        mk_nmr[jj] = sG.reserve()

                        def f_rstd(g, jj=jj, pjj=pjj, w=mk_lnsq[jj],
                                   wa=(mk_apply[jj - 2] if jj >= 2 else None),
                                   m0=m_veps, m1=m_rstd):
                            g.wait_ge(sSc.sem, w)
                            if wa is not None:
                                g.wait_ge(sSc.sem, wa)
                            g.tensor_scalar(stat(VEPS, pjj), stat(SUMSQ, pjj), 1.0 / D, EPS,
                                            op0=ALU.mult, op1=ALU.add).then_inc(sG.sem, 1)
                            g.wait_ge(sG.sem, m0)
                            g.tensor_tensor(stat(RSTD, pjj), stat(VEPS, pjj), neghalf[:],
                                            op=ALU.pow).then_inc(sG.sem, 1)
                            g.wait_ge(sG.sem, m1)
                            g.tensor_tensor(stat(NMR, pjj), stat(NMEAN, pjj), stat(RSTD, pjj),
                                            op=ALU.mult).then_inc(sG.sem, 1)
                        sch.g(f_rstd)

                        mk_apply[jj] = sSc.reserve()

                        def f_apply(sce, jj=jj, pjj=pjj, w=mk_nmr[jj],
                                    wsp=(x2_spill[jj - 2] if jj >= 2 else None)):
                            sce.wait_ge(sG.sem, w)
                            if wsp is not None:
                                sce.wait_ge(sGD.sem, wsp)
                            sce.activation(x2o[:, pjj * D:(pjj + 1) * D],
                                           ot[:, pjj * D:(pjj + 1) * D],
                                           AF.Identity, bias=stat(NMR, pjj),
                                           scale=stat(RSTD, pjj)).then_inc(sSc.sem, 1)
                        sch.s(f_apply)

                        x2_spill[jj] = sGD.reserve(16)
                        if debug and s == 0:
                            sGD.n += 16

                        def f_spill(g, s=s, jj=jj, pjj=pjj, w=mk_apply[jj], dbg_on=(debug and s == 0)):
                            g.wait_ge(sSc.sem, w)
                            g.dma_start(out=x2_d[jj * P:(jj + 1) * P, :],
                                        in_=x2o[:, pjj * D:(pjj + 1) * D]).then_inc(sGD.sem, 16)
                            if dbg_on:
                                g.dma_start(out=dbg["x2"][jj * P:(jj + 1) * P, :],
                                            in_=x2o[:, pjj * D:(pjj + 1) * D]).then_inc(sGD.sem, 16)
                        sch.g(f_spill)

                if s == 0:
                    wp_cast_done[0] = (wj_cs[ND - 1], wj_cs[2 * ND - 1], wj_cs[-1])
                sch.flush(blk)

            with ExitStack() as fx:
                ef = fx.enter_context
                w8g = ef(nc.sbuf_tensor(f"w8g{s}", [P, ND * FI], FP8))
                w8u = ef(nc.sbuf_tensor(f"w8u{s}", [P, ND * FI], FP8))
                w8d = ef(nc.sbuf_tensor(f"w8d{s}", [P, NI * D], FP8))
                x2s = ef(nc.sbuf_tensor(f"x2s{s}", [P, 3 * D], F32))
                tqf = ef(nc.sbuf_tensor(f"tqf{s}", [P, D], F32))
                xq = ef(nc.sbuf_tensor(f"xq{s}", [P, D], BF16))
                xqT = ef(nc.sbuf_tensor(f"xqT{s}", [P, D], BF16))
                rbuf = ef(nc.sbuf_tensor(f"rbuf{s}", [P, 2 * 1024], F32))
                tbuf = ef(nc.sbuf_tensor(f"tbuf{s}", [P, 1024], F32))
                hbuf = ef(nc.sbuf_tensor(f"hbuf{s}", [P, FI], F32))
                thq = ef(nc.sbuf_tensor(f"thq{s}", [P, 2 * 1024], F32))
                hqb = ef(nc.sbuf_tensor(f"hqb{s}", [P, FI], BF16))
                hqT = ef(nc.sbuf_tensor(f"hqT{s}", [P, FI], BF16))
                hst = ef(nc.sbuf_tensor(f"hst{s}", [P, 4], F32))
                ot2 = ef(nc.sbuf_tensor(f"ot2{s}", [P, 2 * D], F32))
                junk2 = ef(nc.sbuf_tensor(f"junk2{s}", [P, D], F32))
                yb = ef(nc.sbuf_tensor(f"yb{s}", [P, 2 * D], F32))
                gp = ef(nc.psum_tensor(f"gp{s}", [P, 1024], F32))
                up = ef(nc.psum_tensor(f"up{s}", [P, 1024], F32))
                dp = ef(nc.psum_tensor(f"dp{s}", [P, 1024], F32))
                trb = ef(nc.psum_tensor(f"trb{s}", [P, 1024], BF16))
                trb2 = ef(nc.psum_tensor(f"trb2{s}", [P, 1024], BF16))
                blk = ef(nc.Block(f"ffn{s}"))
                sch = Sched()
                tpu = []

                prep_done = wp_cast_done[0] if s == 0 else (None, None, None)
                wgu_ready = sLD.reserve(48)
                wd_ready = wgu_ready

                def f_reload(sy, s=s, pg=prep_done[0] if s == 0 else None,
                             pd=prep_done[2] if s == 0 else None):
                    if pg is not None:
                        sy.wait_ge(sGD.sem, pg)
                    sy.dma_start(out=w8g[:], in_=wgT8_d).then_inc(sLD.sem, 16)
                    if pd is not None:
                        sy.wait_ge(sGD.sem, pd)
                    sy.dma_start(out=w8d[:], in_=wdT8_d).then_inc(sLD.sem, 16)
                sch.sy(f_reload)

                def f_reload2(sce, s=s, pu=prep_done[1] if s == 0 else None):
                    if pu is not None:
                        sce.wait_ge(sGD.sem, pu)
                    sce.dma_start(out=w8u[:], in_=wuT8_d).then_inc(sLD.sem, 16)
                sch.s(f_reload2)

                x2_mk = {}
                for jj in range(2):
                    x2_mk[jj] = sGD.reserve(16)

                    def f_x2p(g, jj=jj, w=x2_spill[jj]):
                        g.wait_ge(sGD.sem, w)
                        g.dma_start(out=x2s[:, (jj % 3) * D:(jj % 3 + 1) * D],
                                    in_=x2_d[jj * P:(jj + 1) * P, :]).then_inc(sGD.sem, 16)
                    sch.g(f_x2p)

                mk_sa = [None] * NQT
                mk_tq = [None] * NQT
                mk_xq = [None] * NQT
                mk_xqt = [None] * NQT
                mk_xqte = [None] * NQT
                mk_gate = [[None] * 4 for _ in range(NQT)]
                mk_up = [[None] * 4 for _ in range(NQT)]
                mk_r = [[None] * 4 for _ in range(NQT)]
                mk_t1 = [[None] * 4 for _ in range(NQT)]
                mk_h = [[None] * 4 for _ in range(NQT)]
                mk_seff = [None] * NQT
                mk_thq = [None] * NQT
                mk_hq = [None] * NQT
                mk_hqt = [[None] * 4 for _ in range(NQT)]
                mk_hqte = [[None] * 4 for _ in range(NQT)]
                mk_down = [None] * NQT
                mk_ydt = [None] * NQT
                mk_lnsq2 = [None] * NQT
                mk_nmr2 = [None] * NQT
                mk_apply2 = [None] * NQT
                mk_yst = [None] * NQT
                mk_nmean2 = [None] * NQT
                mk_veps2 = [None] * NQT

                def sa_ops(jj):
                    pjj = jj % 2
                    return [
                        lambda v: v.tensor_reduce(stat(AMAX, pjj), x2s[:, (jj % 3) * D:(jj % 3 + 1) * D],
                                                  axis=AX, op=ALU.max, apply_absolute_value=True),
                        lambda v: v.tensor_scalar_max(stat(C1, pjj), stat(AMAX, pjj), EPS),
                        lambda v: v.reciprocal(stat(SA, pjj), stat(C1, pjj)),
                        lambda v: v.tensor_scalar_mul(stat(SA, pjj), stat(SA, pjj), 127.0),
                        lambda v: v.tensor_scalar_mul(stat(T0, pjj), stat(C1, pjj), 1.0 / 127.0),
                        lambda v: v.tensor_tensor(stat(T1, pjj), stat(T0, pjj), stat(T0, pjj), op=ALU.mult),
                        lambda v: v.tensor_tensor(stat(CGHU, pjj), stat(T1, pjj), stat(T0, pjj), op=ALU.mult),
                        lambda v: v.tensor_tensor(stat(CGHU, pjj), stat(CGHU, pjj), bc_sb[:, 0:1], op=ALU.mult),
                    ]

                def em_sa(jn):
                    mk_sa[jn] = chain(sch.v, sV, sa_ops(jn), pre=[(sGD.sem, x2_mk[jn])])

                def em_tq(jn):
                    mk_tq[jn] = sSc.reserve()

                    def f_tq(sce, jn=jn, wsa=mk_sa[jn],
                             wxq=(mk_xq[jn - 1] if jn >= 1 else None)):
                        sce.wait_ge(sV.sem, wsa)
                        if wxq is not None:
                            sce.wait_ge(sV.sem, wxq)
                        sce.activation(tqf[:], x2s[:, (jn % 3) * D:(jn % 3 + 1) * D],
                                       AF.Identity, bias=crne_b[:],
                                       scale=stat(SA, jn % 2)).then_inc(sSc.sem, 1)
                    sch.s(f_tq)

                def em_sa_tq(jn):
                    em_sa(jn)
                    em_tq(jn)

                def em_xq(jn):
                    mk_xq[jn] = sV.reserve()

                    def f_xq(v, jn=jn, w=mk_tq[jn],
                             wt=(mk_xqt[jn - 1] if jn >= 1 else None)):
                        v.wait_ge(sSc.sem, w)
                        if wt is not None:
                            v.wait_ge(sPE.sem, wt)
                        v.tensor_scalar_add(xq[:], tqf[:], -CRNE).then_inc(sV.sem, 1)
                    sch.v(f_xq)

                def em_xqt(jn):
                    mk_xqt[jn] = sPE.reserve()
                    tb = trb if len(tpu) % 2 == 0 else trb2
                    tguard = tpu[-2] if len(tpu) >= 2 else None

                    def f_xqt(t, jn=jn, w=mk_xq[jn], tb=tb, tguard=tguard):
                        t.wait_ge(sV.sem, w)
                        if tguard is not None:
                            t.wait_ge(tguard[0].sem, tguard[1])
                        for dc in range(ND):
                            ins = t.transpose(tb[:, dc * P:(dc + 1) * P],
                                              xq[:, dc * P:(dc + 1) * P], ident[:])
                        ins.then_inc(sPE.sem, 1)
                    sch.t(f_xqt)
                    mk_xqte[jn] = sSc.reserve()

                    def f_xqte(sce, jn=jn, w=mk_xqt[jn], tb=tb,
                               wp=(mk_up[jn - 1][3] if jn >= 1 else None)):
                        sce.wait_ge(sPE.sem, w)
                        if wp is not None:
                            sce.wait_ge(sPE.sem, wp)
                        sce.activation(xqT[:], tb[:], AF.Identity).then_inc(sSc.sem, 1)
                    sch.s(f_xqte)
                    tpu.append((sSc, mk_xqte[jn]))

                def em_guq(jn, q):
                    mk_gate[jn][q] = sPE.reserve()
                    mk_up[jn][q] = sPE.reserve()
                    prev_jq = None if (jn == 0 and q == 0) else ((jn, q - 1) if q > 0 else (jn - 1, 3))

                    def f_gu(t, jn=jn, q=q, wx=mk_xqte[jn],
                             wr=(mk_r[prev_jq[0]][prev_jq[1]] if prev_jq else None),
                             wt1=(mk_t1[prev_jq[0]][prev_jq[1]] if prev_jq else None),
                             wld=(wgu_ready if (jn == 0 and q == 0) else None)):
                        t.wait_ge(sSc.sem, wx)
                        if wld is not None:
                            t.wait_ge(sLD.sem, wld)
                        if wr is not None:
                            t.wait_ge(sSc.sem, wr)
                        for ih in range(2):
                            for dc in range(ND):
                                ins = t.matmul(
                                    gp[:, ih * 512:(ih + 1) * 512],
                                    xqT[:, dc * P:(dc + 1) * P],
                                    w8g[:, dc * FI + q * 1024 + ih * 512: dc * FI + q * 1024 + (ih + 1) * 512],
                                    start=(dc == 0), stop=(dc == ND - 1))
                        ins.then_inc(sPE.sem, 1)
                        if wt1 is not None:
                            t.wait_ge(sV.sem, wt1)
                        for ih in range(2):
                            for dc in range(ND):
                                ins = t.matmul(
                                    up[:, ih * 512:(ih + 1) * 512],
                                    xqT[:, dc * P:(dc + 1) * P],
                                    w8u[:, dc * FI + q * 1024 + ih * 512: dc * FI + q * 1024 + (ih + 1) * 512],
                                    start=(dc == 0), stop=(dc == ND - 1))
                        ins.then_inc(sPE.sem, 1)
                    sch.t(f_gu)

                def em_relu(jn, q):
                    mk_r[jn][q] = sSc.reserve()
                    hprev = (jn, q - 2) if q >= 2 else (jn - 1, q + 2)

                    def f_relu(sce, jn=jn, q=q, wg=mk_gate[jn][q],
                               wh=(mk_h[hprev[0]][hprev[1]] if hprev[0] >= 0 else None)):
                        sce.wait_ge(sPE.sem, wg)
                        if wh is not None:
                            sce.wait_ge(sV.sem, wh)
                        sce.activation(rbuf[:, (q % 2) * 1024:(q % 2 + 1) * 1024],
                                       gp[:], AF.Relu).then_inc(sSc.sem, 1)
                    sch.s(f_relu)

                def em_hh(jn, q, with_stat=True):
                    mk_t1[jn][q] = sV.reserve()
                    m_h1 = sV.reserve()
                    mk_h[jn][q] = sV.reserve() if with_stat else m_h1

                    def f_h(v, jn=jn, q=q, wr=mk_r[jn][q], wu=mk_up[jn][q],
                            wthq=(mk_thq[jn - 1] if (jn >= 1 and q == 0) else None),
                            m1=mk_t1[jn][q], m2=m_h1, with_stat=with_stat):
                        v.wait_ge(sSc.sem, wr)
                        v.wait_ge(sPE.sem, wu)
                        v.tensor_tensor(tbuf[:], up[:],
                                        rbuf[:, (q % 2) * 1024:(q % 2 + 1) * 1024],
                                        op=ALU.mult).then_inc(sV.sem, 1)
                        if wthq is not None:
                            v.wait_ge(sSc.sem, wthq)
                        v.wait_ge(sV.sem, m1)
                        v.tensor_tensor(hbuf[:, q * 1024:(q + 1) * 1024], tbuf[:],
                                        rbuf[:, (q % 2) * 1024:(q % 2 + 1) * 1024],
                                        op=ALU.mult).then_inc(sV.sem, 1)
                        if with_stat:
                            v.wait_ge(sV.sem, m2)
                            v.tensor_reduce(hst[:, q:q + 1],
                                            hbuf[:, q * 1024:(q + 1) * 1024],
                                            axis=AX, op=ALU.max,
                                            apply_absolute_value=True).then_inc(sV.sem, 1)
                    sch.v(f_h)

                em_sa_tq(0)
                em_xq(0)
                em_xqt(0)
                em_guq(0, 0)
                em_relu(0, 0)
                em_hh(0, 0)

                if debug and s == 0:
                    def f_dbg1(g, w=mk_xq[0], w2=wd_ready):
                        g.wait_ge(sV.sem, w)
                        g.dma_start(out=dbg["xq0"], in_=xq[:]).then_inc(sGD.sem, 16)
                        g.wait_ge(sLD.sem, w2)
                        g.dma_start(out=dbg["w8g"], in_=w8g[:, 0:FI]).then_inc(sGD.sem, 16)
                        g.dma_start(out=dbg["w8d"], in_=w8d[:, 0:D]).then_inc(sGD.sem, 16)
                    sch.g(f_dbg1)
                    sGD.n += 48

                for j in range(NQT):
                    pj = j % 2

                    if j + 1 < NQT:
                        em_sa(j + 1)

                    for q in (1, 2, 3):
                        em_guq(j, q)
                        em_relu(j, q)
                        em_hh(j, q, with_stat=(q != 3))
                        if q == 1 and j + 1 < NQT:
                            em_tq(j + 1)
                        if q == 2 and j + 1 < NQT:
                            em_xq(j + 1)

                    if j + 1 < NQT:
                        em_xqt(j + 1)

                    mk_seff[j] = chain(sch.v, sV, [
                        (lambda v, j=j: v.tensor_reduce(hst[:, 3:4], hbuf[:, 3 * 1024:4 * 1024],
                                                        axis=AX, op=ALU.max, apply_absolute_value=True)),
                        (lambda v, pj=pj: v.tensor_reduce(stat(HMAX, pj), hst[:, 0:4], axis=AX, op=ALU.max)),
                        (lambda v, pj=pj: v.tensor_tensor(stat(BCLIP, pj), stat(CGHU, pj), stat(HMAX, pj), op=ALU.mult)),
                        (lambda v, pj=pj: v.tensor_scalar_max(stat(BCLIP, pj), stat(BCLIP, pj), EPS)),
                        (lambda v, pj=pj: v.reciprocal(stat(SEFF, pj), stat(BCLIP, pj))),
                        (lambda v, pj=pj: v.tensor_tensor(stat(SEFF, pj), stat(SEFF, pj), stat(CGHU, pj), op=ALU.mult)),
                        (lambda v, pj=pj: v.tensor_scalar_mul(stat(SEFF, pj), stat(SEFF, pj), 127.0)),
                        (lambda v, pj=pj: v.tensor_tensor(stat(CDOWN, pj), stat(BCLIP, pj), bc_sb[:, 1:2], op=ALU.mult)),
                    ], pre=[(sV.sem, mk_h[j][3])])

                    thq_mks = [sSc.reserve() for _ in range(4)]
                    hq_mks = [sV.reserve() for _ in range(4)]

                    def f_thq(sce, j=j, pj=pj, wseff=mk_seff[j], hm=hq_mks,
                              whqprev=(mk_hq[j - 1] if j >= 1 else None)):
                        sce.wait_ge(sV.sem, wseff)
                        if whqprev is not None:
                            sce.wait_ge(sV.sem, whqprev)
                        for cq in range(4):
                            if cq >= 2:
                                sce.wait_ge(sV.sem, hm[cq - 2])
                            sce.activation(thq[:, (cq % 2) * 1024:(cq % 2 + 1) * 1024],
                                           hbuf[:, cq * 1024:(cq + 1) * 1024],
                                           AF.Identity, bias=crne_b[:],
                                           scale=stat(SEFF, pj)).then_inc(sSc.sem, 1)
                    sch.s(f_thq)

                    def f_hq(v, j=j, tm=thq_mks, wtr=(mk_hqt[j - 1][3] if j >= 1 else None)):
                        if wtr is not None:
                            v.wait_ge(sPE.sem, wtr)
                        for cq in range(4):
                            v.wait_ge(sSc.sem, tm[cq])
                            v.tensor_scalar_add(hqb[:, cq * 1024:(cq + 1) * 1024],
                                                thq[:, (cq % 2) * 1024:(cq % 2 + 1) * 1024],
                                                -CRNE).then_inc(sV.sem, 1)
                    sch.v(f_hq)
                    mk_thq[j] = thq_mks[3]
                    mk_hq[j] = hq_mks[3]

                    if debug and s == 0 and j == 0:
                        def f_dbg2(g, w=mk_hq[0]):
                            g.wait_ge(sV.sem, w)
                            g.dma_start(out=dbg["hq0"], in_=hqb[:]).then_inc(sGD.sem, 16)
                            g.dma_start(out=dbg["h0"], in_=hbuf[:]).then_inc(sGD.sem, 16)
                        sch.g(f_dbg2)
                        sGD.n += 32

                    if j + 1 < NQT:
                        em_guq(j + 1, 0)

                    for grp in range(4):
                        mk_hqt[j][grp] = sPE.reserve()
                        tb = trb if len(tpu) % 2 == 0 else trb2
                        tguard = tpu[-2] if len(tpu) >= 2 else None

                        def f_hqt(t, j=j, grp=grp, whq=hq_mks[grp], tb=tb, tguard=tguard):
                            t.wait_ge(sV.sem, whq)
                            if tguard is not None:
                                t.wait_ge(tguard[0].sem, tguard[1])
                            for ic in range(8):
                                icc = grp * 8 + ic
                                ins = t.transpose(tb[:, ic * P:(ic + 1) * P],
                                                  hqb[:, icc * P:(icc + 1) * P], ident[:])
                            ins.then_inc(sPE.sem, 1)
                        sch.t(f_hqt)

                        if grp % 2 == 0:
                            mk_hqte[j][grp] = sSc.reserve()

                            def f_hqte(sce, j=j, grp=grp, w=mk_hqt[j][grp], tb=tb,
                                       wd=(mk_down[j - 1] if (j >= 1 and grp == 0) else None)):
                                sce.wait_ge(sPE.sem, w)
                                if wd is not None:
                                    sce.wait_ge(sPE.sem, wd)
                                sce.activation(hqT[:, grp * 1024:(grp + 1) * 1024],
                                               tb[:], AF.Identity).then_inc(sSc.sem, 1)
                            sch.s(f_hqte)
                            tpu.append((sSc, mk_hqte[j][grp]))
                        else:
                            mk_hqte[j][grp] = sV.reserve()

                            def f_hqte(v, j=j, grp=grp, w=mk_hqt[j][grp], tb=tb):
                                v.wait_ge(sPE.sem, w)
                                v.tensor_copy(hqT[:, grp * 1024:(grp + 1) * 1024],
                                              tb[:]).then_inc(sV.sem, 1)
                            sch.v(f_hqte)
                            tpu.append((sV, mk_hqte[j][grp]))

                    if j + 1 < NQT:
                        em_relu(j + 1, 0)
                        em_hh(j + 1, 0)

                    mk_down[j] = sPE.reserve()

                    def f_down(t, j=j, we=mk_hqte[j][3], we2=mk_hqte[j][2],
                               we1=mk_hqte[j][1],
                               wy=(mk_ydt[j - 1] if j >= 1 else None),
                               wld=(wd_ready if j == 0 else None)):
                        t.wait_ge(sV.sem, we1)
                        t.wait_ge(sSc.sem, we2)
                        if wld is not None:
                            t.wait_ge(sLD.sem, wld)
                        if wy is not None:
                            t.wait_ge(sV.sem, wy)
                        for b in range(2):
                            for ic in range(24):
                                t.matmul(dp[:, b * 512:(b + 1) * 512],
                                         hqT[:, ic * P:(ic + 1) * P],
                                         w8d[:, ic * D + b * 512: ic * D + (b + 1) * 512],
                                         start=(ic == 0), stop=False)
                        t.wait_ge(sV.sem, we)
                        for b in range(2):
                            for ic in range(24, NI):
                                ins = t.matmul(dp[:, b * 512:(b + 1) * 512],
                                               hqT[:, ic * P:(ic + 1) * P],
                                               w8d[:, ic * D + b * 512: ic * D + (b + 1) * 512],
                                               start=False, stop=(ic == NI - 1))
                        ins.then_inc(sPE.sem, 1)
                    sch.t(f_down)

                    mk_ydt[j] = sV.reserve()
                    m_ms = sV.reserve()
                    mk_nmean2[j] = sV.reserve()

                    def f_ydt(v, j=j, pj=pj, wd=mk_down[j],
                              wa=(mk_apply2[j - 2] if j >= 2 else None),
                              wn=(mk_nmr2[j - 2] if j >= 2 else None),
                              m1=mk_ydt[j], m2=m_ms):
                        v.wait_ge(sPE.sem, wd)
                        if wa is not None:
                            v.wait_ge(sSc.sem, wa)
                        v.scalar_tensor_tensor(ot2[:, pj * D:(pj + 1) * D], dp[:],
                                               stat(CDOWN, pj), x2s[:, (j % 3) * D:(j % 3 + 1) * D],
                                               op0=ALU.mult, op1=ALU.add).then_inc(sV.sem, 1)
                        if wn is not None:
                            v.wait_ge(sG.sem, wn)
                        v.wait_ge(sV.sem, m1)
                        v.tensor_reduce(stat(SUMSQ, pj), ot2[:, pj * D:(pj + 1) * D],
                                        axis=AX, op=ALU.add).then_inc(sV.sem, 1)
                        v.wait_ge(sV.sem, m2)
                        v.tensor_scalar_mul(stat(NMEAN, pj), stat(SUMSQ, pj),
                                            -1.0 / D).then_inc(sV.sem, 1)
                    sch.v(f_ydt)

                    if j + 2 < NQT:
                        x2_mk[j + 2] = sGD.reserve(16)

                        def f_x2n(g, j=j, wy=(mk_ydt[j - 1] if j >= 1 else None), wsp=x2_spill[j + 2]):
                            if wy is not None:
                                g.wait_ge(sV.sem, wy)
                            g.wait_ge(sGD.sem, wsp)
                            g.dma_start(out=x2s[:, ((j + 2) % 3) * D:((j + 2) % 3 + 1) * D],
                                        in_=x2_d[(j + 2) * P:(j + 3) * P, :]).then_inc(sGD.sem, 16)
                        sch.g(f_x2n)

                    for jj in ([j - 1] if j >= 1 else []) + ([NQT - 1] if j == NQT - 1 else []):
                        pjj = jj % 2

                        mk_lnsq2[jj] = sSc.reserve()

                        def f_lnsq(sce, jj=jj, pjj=pjj, w=mk_nmean2[jj]):
                            sce.wait_ge(sV.sem, w)
                            sce.activation(junk2[:], ot2[:, pjj * D:(pjj + 1) * D], AF.Square,
                                           bias=stat(NMEAN, pjj), scale=1.0,
                                           accum_out=stat(SUMSQ, pjj)).then_inc(sSc.sem, 1)
                        sch.s(f_lnsq)

                        m_veps = sG.reserve()
                        m_rstd = sG.reserve()
                        mk_nmr2[jj] = sG.reserve()

                        def f_rstd(g, jj=jj, pjj=pjj, w=mk_lnsq2[jj],
                                   wa=(mk_apply2[jj - 2] if jj >= 2 else None),
                                   m0=m_veps, m1=m_rstd):
                            g.wait_ge(sSc.sem, w)
                            if wa is not None:
                                g.wait_ge(sSc.sem, wa)
                            g.tensor_scalar(stat(VEPS, pjj), stat(SUMSQ, pjj), 1.0 / D, EPS,
                                            op0=ALU.mult, op1=ALU.add).then_inc(sG.sem, 1)
                            g.wait_ge(sG.sem, m0)
                            g.tensor_tensor(stat(RSTD, pjj), stat(VEPS, pjj), neghalf[:],
                                            op=ALU.pow).then_inc(sG.sem, 1)
                            g.wait_ge(sG.sem, m1)
                            g.tensor_tensor(stat(NMR, pjj), stat(NMEAN, pjj), stat(RSTD, pjj),
                                            op=ALU.mult).then_inc(sG.sem, 1)
                        sch.g(f_rstd)

                        mk_apply2[jj] = sSc.reserve()

                        def f_apply(sce, jj=jj, pjj=pjj, w=mk_nmr2[jj],
                                    wy=(mk_yst[jj - 2] if jj >= 2 else None)):
                            sce.wait_ge(sG.sem, w)
                            if wy is not None:
                                sce.wait_ge(sGD.sem, wy)
                            sce.activation(yb[:, pjj * D:(pjj + 1) * D],
                                           ot2[:, pjj * D:(pjj + 1) * D],
                                           AF.Identity, bias=stat(NMR, pjj),
                                           scale=stat(RSTD, pjj)).then_inc(sSc.sem, 1)
                        sch.s(f_apply)

                        mk_yst[jj] = sGD.reserve(16)

                        def f_yst(g, jj=jj, pjj=pjj, w=mk_apply2[jj]):
                            g.wait_ge(sSc.sem, w)
                            g.dma_start(out=y_d[s, jj * P:(jj + 1) * P, :],
                                        in_=yb[:, pjj * D:(pjj + 1) * D]).then_inc(sGD.sem, 16)
                        sch.g(f_yst)

                if s == 0:
                    emit_xstage(sch, 1)
                sch.flush(blk)

        with nc.Block("fin") as blk:
            @blk.gpsimd
            def _(g):
                g.wait_ge(sGD.sem, sGD.n)

            @blk.sync
            def _(sy):
                sy.wait_ge(sST.sem, sST.n)

    return nc


_NC_CACHE = {}


def kernel(L0_attn_x, L0_attn_Wq, L0_attn_bq, L0_attn_Wk, L0_attn_bk,
           L0_ffn_w_gate, L0_ffn_w_up, L0_ffn_w_down):
    x = np.ascontiguousarray(np.asarray(L0_attn_x, dtype=np.float32))
    Wq = np.ascontiguousarray(np.asarray(L0_attn_Wq, dtype=np.float32))
    bq = np.ascontiguousarray(np.asarray(L0_attn_bq, dtype=np.float32))
    Wk = np.ascontiguousarray(np.asarray(L0_attn_Wk, dtype=np.float32))
    bk = np.ascontiguousarray(np.asarray(L0_attn_bk, dtype=np.float32))
    Wg = np.ascontiguousarray(np.asarray(L0_ffn_w_gate, dtype=np.float32))
    Wu = np.ascontiguousarray(np.asarray(L0_ffn_w_up, dtype=np.float32))
    Wd = np.ascontiguousarray(np.asarray(L0_ffn_w_down, dtype=np.float32))

    if "nc" not in _NC_CACHE:
        _NC_CACHE["nc"] = build_program()
    nc = _NC_CACHE["nc"]

    in_maps = []
    for c in range(NCORES):
        in_maps.append({
            "x": x[NSEQ * c: NSEQ * (c + 1)],
            "Wq": Wq, "bq": bq, "Wk": Wk, "bk": bk,
            "Wg": Wg, "Wu": Wu, "Wd": Wd,
        })
    res = run_bass_kernel_spmd(nc, in_maps, core_ids=list(range(NCORES)))
    out = np.concatenate([res.results[c]["y"] for c in range(NCORES)], axis=0)
    return out


# version 35 (final)
# speedup vs baseline: 1.0062x; 1.0062x over previous
import numpy as np
import ml_dtypes
from contextlib import ExitStack

import concourse.bass as bass
import concourse.mybir as mybir
from concourse.bass_utils import run_bass_kernel_spmd

F32 = mybir.dt.float32
BF16 = mybir.dt.bfloat16
FP8 = mybir.dt.float8e4
AX = mybir.AxisListType.X
AF = mybir.ActivationFunctionType
ALU = mybir.AluOpType

P = 128
S = 2048
D = 1024
FI = 4096
NSEQ = 2
NQT = S // P
ND = D // P
NI = FI // P
CRNE = 3.0 * 2.0**22
EPS = 1e-5
ISQD = 1.0 / 32.0
NCORES = 8


class Ctr:
    def __init__(self, sem):
        self.sem = sem
        self.n = 0

    def inc(self, instr, k=1):
        instr.then_inc(self.sem, k)
        self.n += k
        return self.n

    def reserve(self, k=1):
        self.n += k
        return self.n


class Sched:

    ORDER = ("sync", "gpsimd", "tensor", "vector", "scalar")

    def __init__(self):
        self.q = {k: [] for k in self.ORDER}

    def t(self, fn):
        self.q["tensor"].append(fn)

    def v(self, fn):
        self.q["vector"].append(fn)

    def s(self, fn):
        self.q["scalar"].append(fn)

    def g(self, fn):
        self.q["gpsimd"].append(fn)

    def sy(self, fn):
        self.q["sync"].append(fn)

    def flush(self, blk):
        for name in self.ORDER:
            fns = self.q[name]
            if not fns:
                continue

            def body(eng, fns=fns):
                for f in fns:
                    f(eng)
            getattr(blk, name)(body)
        self.q = {k: [] for k in self.ORDER}


def chain(addfn, ctr, ops, pre=()):
    marks = [ctr.reserve() for _ in ops]

    def f(e, ops=list(ops), marks=marks, sem=ctr.sem, pre=list(pre)):
        for ps, pv in pre:
            e.wait_ge(ps, pv)
        for i, opf in enumerate(ops):
            ins = opf(e)
            ins.then_inc(sem, 1)
            if i + 1 < len(ops):
                e.wait_ge(sem, marks[i])
    addfn(f)
    return marks[-1]


def build_program(debug=False):
    nc = bass.Bass("TRN2", target_bir_lowering=False)
    dbg = {}
    if debug:
        dbg["A"] = nc.dram_tensor("dbg_A", [P, ND * D], BF16, kind="ExternalOutput").ap()
        dbg["c2"] = nc.dram_tensor("dbg_c2", [P, ND], BF16, kind="ExternalOutput").ap()
        dbg["qT"] = nc.dram_tensor("dbg_qT", [P, ND * S], BF16, kind="ExternalOutput").ap()
        dbg["xT"] = nc.dram_tensor("dbg_xT", [P, ND * S], BF16, kind="ExternalOutput").ap()
        dbg["crow"] = nc.dram_tensor("dbg_crow", [1, D], BF16, kind="ExternalOutput").ap()
        dbg["vrow"] = nc.dram_tensor("dbg_vrow", [1, S], BF16, kind="ExternalOutput").ap()
        dbg["E0"] = nc.dram_tensor("dbg_E0", [P, S], F32, kind="ExternalOutput").ap()
        dbg["P0"] = nc.dram_tensor("dbg_P0", [P, S], BF16, kind="ExternalOutput").ap()
        dbg["st"] = nc.dram_tensor("dbg_st", [P, 40], F32, kind="ExternalOutput").ap()
        dbg["x2"] = nc.dram_tensor("dbg_x2", [S, D], F32, kind="ExternalOutput").ap()
        dbg["xq0"] = nc.dram_tensor("dbg_xq0", [P, D], BF16, kind="ExternalOutput").ap()
        dbg["hq0"] = nc.dram_tensor("dbg_hq0", [P, FI], BF16, kind="ExternalOutput").ap()
        dbg["h0"] = nc.dram_tensor("dbg_h0", [P, FI], F32, kind="ExternalOutput").ap()
        dbg["w8g"] = nc.dram_tensor("dbg_w8g", [P, FI], FP8, kind="ExternalOutput").ap()
        dbg["w8d"] = nc.dram_tensor("dbg_w8d", [P, D], FP8, kind="ExternalOutput").ap()

    x_d = nc.dram_tensor("x", [NSEQ, S, D], F32, kind="ExternalInput").ap()
    wq_d = nc.dram_tensor("Wq", [D, D], F32, kind="ExternalInput").ap()
    bq_d = nc.dram_tensor("bq", [D], F32, kind="ExternalInput").ap()
    wk_d = nc.dram_tensor("Wk", [D, D], F32, kind="ExternalInput").ap()
    bk_d = nc.dram_tensor("bk", [D], F32, kind="ExternalInput").ap()
    wg_d = nc.dram_tensor("Wg", [FI, D], F32, kind="ExternalInput").ap()
    wu_d = nc.dram_tensor("Wu", [FI, D], F32, kind="ExternalInput").ap()
    wd_d = nc.dram_tensor("Wd", [D, FI], F32, kind="ExternalInput").ap()
    y_d = nc.dram_tensor("y", [NSEQ, S, D], F32, kind="ExternalOutput").ap()

    xbf_d = nc.dram_tensor("xbf_scr", [S, D], BF16, kind="Internal").ap()
    x2_d = nc.dram_tensor("x2_scr", [S, D], F32, kind="Internal").ap()
    wgT8_d = nc.dram_tensor("wgT8_scr", [P, ND * FI], FP8, kind="Internal").ap()
    wuT8_d = nc.dram_tensor("wuT8_scr", [P, ND * FI], FP8, kind="Internal").ap()
    wdT8_d = nc.dram_tensor("wdT8_scr", [P, NI * D], FP8, kind="Internal").ap()
    a_scr = nc.dram_tensor("a_scr", [P, ND * D], BF16, kind="Internal").ap()
    wqbf_g = nc.dram_tensor("wqbf_g", [FI, D], BF16, kind="Internal").ap()
    wqbf_u = nc.dram_tensor("wqbf_u", [FI, D], BF16, kind="Internal").ap()
    wqbf_d = nc.dram_tensor("wqbf_d", [D, FI], BF16, kind="Internal").ap()

    ident_c = nc.inline_tensor(np.eye(P, dtype=ml_dtypes.bfloat16), "ident_c")
    onesr_b_c = nc.inline_tensor(np.ones((1, P), dtype=ml_dtypes.bfloat16), "onesr_b_c")
    onesc_b_c = nc.inline_tensor(np.ones((P, 1), dtype=ml_dtypes.bfloat16), "onesc_b_c")
    onesr_f_c = nc.inline_tensor(np.ones((1, P), dtype=np.float32), "onesr_f_c")
    onesc_f_c = nc.inline_tensor(np.ones((P, 1), dtype=np.float32), "onesc_f_c")
    neghalf_c = nc.inline_tensor(np.full((P, 1), -0.5, dtype=np.float32), "neghalf_c")
    crne_c = nc.inline_tensor(np.full((P, 1), CRNE, dtype=np.float32), "crne_c")

    with ExitStack() as ctx:
        e = ctx.enter_context

        ident = e(nc.sbuf_tensor("ident", [P, P], BF16))
        onesr_b = e(nc.sbuf_tensor("onesr_b", [1, P], BF16))
        onesc_b = e(nc.sbuf_tensor("onesc_b", [P, 1], BF16))
        onesr_f = e(nc.sbuf_tensor("onesr_f", [1, P], F32))
        onesc_f = e(nc.sbuf_tensor("onesc_f", [P, 1], F32))
        neghalf = e(nc.sbuf_tensor("neghalf", [P, 1], F32))
        crne_b = e(nc.sbuf_tensor("crne_b", [P, 1], F32))
        c2_sb = e(nc.sbuf_tensor("c2_sb", [P, ND], BF16))
        c0_sb = e(nc.sbuf_tensor("c0_sb", [1, 1], F32))
        bc_sb = e(nc.sbuf_tensor("bc_sb", [P, 2], F32))
        sc_sb = e(nc.sbuf_tensor("sc_sb", [1, 16], F32))
        sw_sb = e(nc.sbuf_tensor("sw_sb", [P, 1], F32))
        st_sb = e(nc.sbuf_tensor("st_sb", [P, 40], F32))
        stg = e(nc.sbuf_tensor("stg", [P, 8 * 1024], BF16))

        (NEGM, ZS, RCPZ, NMEAN, SUMSQ, VEPS, RSTD, NMR, AMAX, C1, SA,
         CGHU, HMAX, BCLIP, SEFF, CDOWN, T0, T1) = range(18)

        def stat(idx, par):
            c = idx * 2 + par
            return st_sb[:, c:c + 1]

        sLD = Ctr(e(nc.semaphore("sLD")))
        sGD = Ctr(e(nc.semaphore("sGD")))
        sST = Ctr(e(nc.semaphore("sST")))
        sPE = Ctr(e(nc.semaphore("sPE")))
        sV = Ctr(e(nc.semaphore("sV")))
        sSc = Ctr(e(nc.semaphore("sSc")))
        sG = Ctr(e(nc.semaphore("sG")))
        sL2 = Ctr(e(nc.semaphore("sL2")))

        xstage_done = [None, None]
        xT_done = [None, None]

        def emit_xstage(sch_, s):
            casts = [None] * NQT
            spills = [None] * NQT
            for c in range(NQT):
                casts[c] = sGD.reserve(16)
            for c in range(NQT):
                spills[c] = sST.reserve(16)

            def f_cast(g, s=s):
                for c in range(NQT):
                    if c >= 8:
                        g.wait_ge(sST.sem, spills[c - 8])
                    g.dma_start(out=stg[:, (c % 8) * 1024:(c % 8 + 1) * 1024],
                                in_=x_d[s, c * P:(c + 1) * P, :]).then_inc(sGD.sem, 16)
            sch_.g(f_cast)

            def f_spill(sy, s=s, pre=(xT_done[s - 1] if s >= 1 else None)):
                for c in range(NQT):
                    if c == 0 and pre is not None:
                        sy.wait_ge(sLD.sem, pre)
                    sy.wait_ge(sGD.sem, casts[c])
                    sy.dma_start(out=xbf_d[c * P:(c + 1) * P, :],
                                 in_=stg[:, (c % 8) * 1024:(c % 8 + 1) * 1024]).then_inc(sST.sem, 16)
            sch_.sy(f_spill)
            xstage_done[s] = spills[NQT - 1]

        with ExitStack() as p0:
            ep = p0.enter_context
            wqb = ep(nc.sbuf_tensor("wqb", [P, ND * D], BF16))
            wkb = ep(nc.sbuf_tensor("wkb", [P, ND * D], BF16))
            A_sb = ep(nc.sbuf_tensor("A_sb", [P, ND * D], BF16))
            bqb = ep(nc.sbuf_tensor("bqb", [P, ND], BF16))
            bkb = ep(nc.sbuf_tensor("bkb", [P, ND], BF16))
            aps = ep(nc.psum_tensor("aps", [P, 1024], F32))
            cps = ep(nc.psum_tensor("cps", [P, 1], F32))
            c0ps = ep(nc.psum_tensor("c0ps", [1, 1], F32))
            blk = ep(nc.Block("p0"))
            sch0 = Sched()
            emit_xstage(sch0, 0)
            sch0.flush(blk)

            @blk.sync
            def _(sy):
                for dst, src in ((ident, ident_c), (onesr_b, onesr_b_c),
                                 (onesc_b, onesc_b_c), (onesr_f, onesr_f_c),
                                 (onesc_f, onesc_f_c), (neghalf, neghalf_c),
                                 (crne_b, crne_c)):
                    sy.dma_start(out=dst[:], in_=src[:, :]).then_inc(sLD.sem, 16)
                    sLD.n += 16
                sy.wait_ge(sLD.sem, sLD.n)

            @blk.gpsimd
            def _(g):
                g.dma_start(out=wqb[:].rearrange("p (c d) -> p c d", d=D),
                            in_=wq_d.rearrange("(c p) d -> p c d", p=P)).then_inc(sGD.sem, 16)
                sGD.n += 16
                g.dma_start(out=wkb[:].rearrange("p (c d) -> p c d", d=D),
                            in_=wk_d.rearrange("(c p) d -> p c d", p=P)).then_inc(sGD.sem, 16)
                sGD.n += 16
                g.dma_start(out=bqb[:], in_=bq_d.rearrange("(c p) -> p c", p=P),
                            allow_slow_non_contiguous=True).then_inc(sGD.sem, 16)
                sGD.n += 16
                g.dma_start(out=bkb[:], in_=bk_d.rearrange("(c p) -> p c", p=P),
                            allow_slow_non_contiguous=True).then_inc(sGD.sem, 16)
                sGD.n += 16
            wk_loaded = sGD.n

            a_pe = []
            a_ev = []
            for k in range(2 * ND):
                dc, eh = k // 2, k % 2

                @blk.tensor
                def _(t, dc=dc, eh=eh, k=k):
                    if k == 0:
                        t.wait_ge(sGD.sem, wk_loaded)
                    if k >= 2:
                        t.wait_ge(sV.sem, a_ev[k - 2])
                    for fc in range(ND):
                        ins = t.matmul(aps[:, eh * 512:(eh + 1) * 512],
                                       wqb[:, fc * D + dc * P: fc * D + (dc + 1) * P],
                                       wkb[:, fc * D + eh * 512: fc * D + (eh + 1) * 512],
                                       start=(fc == 0), stop=(fc == ND - 1))
                    sPE.inc(ins)
                a_pe.append(sPE.n)

                @blk.vector
                def _(v, dc=dc, eh=eh, k=k):
                    v.wait_ge(sPE.sem, a_pe[k])
                    sV.inc(v.tensor_scalar_mul(
                        A_sb[:, dc * D + eh * 512: dc * D + (eh + 1) * 512],
                        aps[:, eh * 512:(eh + 1) * 512], ISQD))
                a_ev.append(sV.n)

            c2_pe = []
            c2_ev = []
            for dc in range(ND):
                @blk.tensor
                def _(t, dc=dc):
                    if dc >= 1:
                        t.wait_ge(sV.sem, c2_ev[dc - 1])
                    for fc in range(ND):
                        ins = t.matmul(cps[:], wkb[:, fc * D + dc * P: fc * D + (dc + 1) * P],
                                       bqb[:, fc:fc + 1], start=(fc == 0), stop=(fc == ND - 1))
                    sPE.inc(ins)
                c2_pe.append(sPE.n)

                @blk.vector
                def _(v, dc=dc):
                    v.wait_ge(sPE.sem, c2_pe[dc])
                    sV.inc(v.tensor_scalar_mul(c2_sb[:, dc:dc + 1], cps[:], ISQD))
                c2_ev.append(sV.n)

            @blk.tensor
            def _(t):
                for fc in range(ND):
                    ins = t.matmul(c0ps[:], bqb[:, fc:fc + 1], bkb[:, fc:fc + 1],
                                   start=(fc == 0), stop=(fc == ND - 1))
                sPE.inc(ins)
            c0_pe = sPE.n

            @blk.vector
            def _(v):
                v.wait_ge(sPE.sem, c0_pe)
                sV.inc(v.tensor_scalar_mul(c0_sb[:], c0ps[:], ISQD))
            p0_v_done = sV.n

            @blk.gpsimd
            def _(g):
                g.wait_ge(sV.sem, p0_v_done)
                g.dma_start(out=a_scr, in_=A_sb[:]).then_inc(sGD.sem, 16)
                sGD.n += 16
                if debug:
                    g.dma_start(out=dbg["A"], in_=A_sb[:]).then_inc(sGD.sem, 16)
                    sGD.n += 16
                    g.dma_start(out=dbg["c2"], in_=c2_sb[:]).then_inc(sGD.sem, 16)
                    sGD.n += 16
            a_spilled = sGD.n

        wp_spill_done = [None]
        wp_cast_done = [None]

        def weight_prep_all():
            fdim = 32 * 1024
            nch = 16
            wspecs = [(wg_d, wqbf_g, FI, D), (wu_d, wqbf_u, FI, D), (wd_d, wqbf_d, D, FI)]
            with ExitStack() as px:
                ep2 = px.enter_context
                wst = ep2(nc.sbuf_tensor("wst", [P, fdim], F32))
                tqp = ep2(nc.sbuf_tensor("tqp", [P, 3 * 2048], F32))
                uqp = ep2(nc.sbuf_tensor("uqp", [P, 2048], F32))
                wqn = ep2(nc.sbuf_tensor("wqn", [P, 4 * 2048], BF16))
                junkw = ep2(nc.sbuf_tensor("junkw", [P, 2048], F32))
                wsm = ep2(nc.sbuf_tensor("wsm", [P, 18], F32))
                tps = ep2(nc.psum_tensor("tps", [P, 2], F32))
                blk = ep2(nc.Block("wp0"))
                sch = Sched()

                s_mk = [[None] * nch for _ in range(3)]
                v_mk = [[None] * nch for _ in range(3)]
                sp_mk = [[None] * nch for _ in range(3)]

                def spill_args(swidx, sk):
                    wq_dram = wspecs[swidx][1]
                    cols = wspecs[swidx][3]
                    par = sk % 4
                    if cols == 1024:
                        outap = wq_dram[2 * sk * P:(2 * sk + 2) * P, :].rearrange(
                            "(t p) d -> p t d", p=P)
                        inap = wqn[:, par * 2048:(par + 1) * 2048].rearrange(
                            "p (t f) -> p t f", f=1024)
                    else:
                        c_, hh = sk // 2, sk % 2
                        outap = wq_dram[c_ * P:(c_ + 1) * P, hh * 2048:(hh + 1) * 2048]
                        inap = wqn[:, par * 2048:(par + 1) * 2048]
                    return outap, inap

                def slab_args(swidx, g4):
                    w_dram = wspecs[swidx][0]
                    rows, cols = wspecs[swidx][2], wspecs[swidx][3]
                    nt4 = (rows // P) // 4
                    outap = wst[:, g4 * 8192:(g4 + 1) * 8192].rearrange(
                        "p (c d) -> p c d", d=cols)
                    inap = w_dram[g4 * nt4 * P:(g4 + 1) * nt4 * P, :].rearrange(
                        "(c p) d -> p c d", p=P)
                    return outap, inap

                for widx, (w_dram, wq_dram, rows, cols) in enumerate(wspecs):
                    ntile = rows // P

                    w_ld4 = [(sLD, sLD.reserve(16)) if g4 % 2 == 0 else (sL2, sL2.reserve(16))
                             for g4 in range(4)]

                    def f_wld(sy, widx=widx,
                              marks=(s_mk[widx - 1] if widx >= 1 else None)):
                        for g4 in (0, 2):
                            if marks is not None:
                                sy.wait_ge(sSc.sem, marks[4 * g4 + 3])
                            outap, inap = slab_args(widx, g4)
                            sy.dma_start(out=outap, in_=inap).then_inc(sLD.sem, 16)
                    sch.sy(f_wld)

                    if widx == 0:
                        def f_wlds0(s):
                            for g4 in (1, 3):
                                outap, inap = slab_args(0, g4)
                                s.dma_start(out=outap, in_=inap).then_inc(sL2.sem, 16)
                        sch.s(f_wlds0)

                    s_abs = sSc.reserve(8)

                    def f_abs(s, w_ld4=w_ld4):
                        for k in range(0, 16, 2):
                            if k % 4 == 0:
                                ctr, mk = w_ld4[k // 4]
                                s.wait_ge(ctr.sem, mk)
                            s.activation(junkw[:], wst[:, k * 2048:(k + 1) * 2048],
                                         AF.Abs, accum_out=wsm[:, k:k + 1]).then_inc(sSc.sem, 1)
                    sch.s(f_abs)

                    v_abs0 = sV.reserve(8)

                    def f_vabs0(v, w_ld4=w_ld4):
                        for k in range(1, 16, 2):
                            if k % 4 == 1:
                                ctr, mk = w_ld4[k // 4]
                                v.wait_ge(ctr.sem, mk)
                            v.reduce_sum(wsm[:, k:k + 1], wst[:, k * 2048:(k + 1) * 2048],
                                         axis=AX, apply_absolute_value=True).then_inc(sV.sem, 1)
                    sch.v(f_vabs0)

                    v_abs = sV.reserve()

                    def f_vabs(v, w=s_abs, wv=v_abs0):
                        v.wait_ge(sSc.sem, w)
                        v.wait_ge(sV.sem, wv)
                        v.reduce_sum(wsm[:, 16:17], wsm[:, 0:16], axis=AX).then_inc(sV.sem, 1)
                    sch.v(f_vabs)

                    pe_tot = sPE.reserve()

                    def f_tot(t, w=v_abs):
                        t.wait_ge(sV.sem, w)
                        t.matmul(tps[:1, 0:1], onesc_f[:], wsm[:, 16:17],
                                 start=True, stop=True).then_inc(sPE.sem, 1)
                    sch.t(f_tot)

                    v_sc = chain(sch.v, sV, [
                        (lambda v, rows=rows, cols=cols: v.tensor_scalar(
                            sc_sb[:, 0:1], tps[:1, 0:1], 1.0 / float(rows * cols), EPS,
                            op0=ALU.mult, op1=ALU.max)),
                        (lambda v: v.reciprocal(sc_sb[:, 1:2], sc_sb[:, 0:1])),
                    ], pre=[(sPE.sem, pe_tot)])

                    pe_b = sPE.reserve()

                    def f_bc(t, w=v_sc):
                        t.wait_ge(sV.sem, w)
                        t.matmul(tps[:, 1:2], onesr_f[:], sc_sb[:, 1:2],
                                 start=True, stop=True).then_inc(sPE.sem, 1)
                    sch.t(f_bc)

                    v_swb = sV.reserve()

                    def f_swb(v, w=pe_b, widx=widx,
                              wprev=(s_mk[widx - 1][nch - 1] if widx >= 1 else None)):
                        v.wait_ge(sPE.sem, w)
                        if wprev is not None:
                            v.wait_ge(sSc.sem, wprev)
                        v.tensor_copy(sw_sb[:], tps[:, 1:2]).then_inc(sV.sem, 1)
                    sch.v(f_swb)

                    for k in range(nch):
                        par = k % 3

                        s_mk[widx][k] = sSc.reserve()

                        def f_qa(s, widx=widx, k=k, par=par, w0=v_swb,
                                 w2=(v_mk[widx][k - 3] if k >= 3 else
                                     (v_mk[widx - 1][nch - 3 + k] if widx >= 1 else None))):
                            if k == 0:
                                s.wait_ge(sV.sem, w0)
                            if w2 is not None:
                                s.wait_ge(sV.sem, w2)
                            s.activation(tqp[:, par * 2048:(par + 1) * 2048],
                                         wst[:, k * 2048:(k + 1) * 2048],
                                         AF.Identity, bias=crne_b[:],
                                         scale=sw_sb[:]).then_inc(sSc.sem, 1)
                            if k in (7, 15) and widx + 1 < 3:
                                outap, inap = slab_args(widx + 1, 1 if k == 7 else 3)
                                s.dma_start(out=outap, in_=inap).then_inc(sL2.sem, 16)
                        sch.s(f_qa)

                        m_u = sV.reserve()
                        v_mk[widx][k] = sV.reserve()

                        sp_mk[widx][k] = sST.reserve(16)

                        parw = k % 4

                        def f_qb(v, widx=widx, k=k, par=par, parw=parw, w=s_mk[widx][k],
                                 m1=m_u,
                                 wsp=(sp_mk[widx][k - 4] if k >= 4 else
                                      (sp_mk[widx - 1][nch - 4 + k] if widx >= 1 else None))):
                            v.wait_ge(sSc.sem, w)
                            if wsp is not None:
                                v.wait_ge(sST.sem, wsp)
                            v.tensor_scalar(uqp[:], tqp[:, par * 2048:(par + 1) * 2048],
                                            CRNE - 1.0, -CRNE,
                                            op0=ALU.max, op1=ALU.add).then_inc(sV.sem, 1)
                            v.wait_ge(sV.sem, m1)
                            v.tensor_scalar_min(wqn[:, parw * 2048:(parw + 1) * 2048],
                                                uqp[:], 1.0).then_inc(sV.sem, 1)
                        sch.v(f_qb)

                        def f_spg(g, widx=widx, k=k, w=v_mk[widx][k]):
                            g.wait_ge(sV.sem, w)
                            outap, inap = spill_args(widx, k)
                            g.dma_start(out=outap, in_=inap).then_inc(sST.sem, 16)
                        sch.g(f_spg)

                    def f_save(v, widx=widx):
                        v.tensor_copy(sc_sb[:, 2 + widx:3 + widx],
                                      sc_sb[:, 0:1]).then_inc(sV.sem, 1)
                    sV.reserve()
                    sch.v(f_save)

                wp_spill_done[0] = sST.n
                sch.flush(blk)
            return

        weight_prep_all()

        with ExitStack() as pf:
            epf = pf.enter_context
            fps = epf(nc.psum_tensor("fps", [P, 2], F32))
            blk = epf(nc.Block("pfold"))

            @blk.vector
            def _(v):
                sV.inc(v.tensor_tensor(sc_sb[:, 5:6], sc_sb[:, 2:3], sc_sb[:, 2:3], op=ALU.mult))
                v.wait_ge(sV.sem, sV.n)
                sV.inc(v.tensor_tensor(sc_sb[:, 5:6], sc_sb[:, 5:6], sc_sb[:, 3:4], op=ALU.mult))
                v.wait_ge(sV.sem, sV.n)
                sV.inc(v.tensor_scalar_mul(sc_sb[:, 6:7], sc_sb[:, 4:5], 1.0 / 127.0))
            v_f = sV.n

            @blk.tensor
            def _(t):
                t.wait_ge(sV.sem, v_f)
                t.matmul(fps[:, 0:1], onesr_f[:], sc_sb[:, 5:6], start=True, stop=True)
                ins = t.matmul(fps[:, 1:2], onesr_f[:], sc_sb[:, 6:7], start=True, stop=True)
                sPE.inc(ins)
            pe_f = sPE.n

            @blk.vector
            def _(v):
                v.wait_ge(sPE.sem, pe_f)
                sV.inc(v.tensor_copy(bc_sb[:, 0:2], fps[:, 0:2]))

        for s in range(NSEQ):
            x2_spill = [None] * NQT

            with ExitStack() as ax:
                ea = ax.enter_context
                xtok = ea(nc.sbuf_tensor(f"xtok{s}", [P, NQT * D], BF16))
                A_loc = ea(nc.sbuf_tensor(f"A_loc{s}", [P, ND * D], BF16))
                xT = ea(nc.sbuf_tensor(f"xT{s}", [P, ND * S], BF16, is_transpose_dest=True))
                qT = ea(nc.sbuf_tensor(f"qT{s}", [P, ND * S], BF16))
                Eb = ea(nc.sbuf_tensor(f"Eb{s}", [P, S], F32))
                Pb = ea(nc.sbuf_tensor(f"Pb{s}", [P, S], BF16))
                PT = ea(nc.sbuf_tensor(f"PT{s}", [P, S], BF16))
                xr = ea(nc.sbuf_tensor(f"xr{s}", [P, 2 * D], F32))
                ot = ea(nc.sbuf_tensor(f"ot{s}", [P, 2 * D], F32))
                junk = ea(nc.sbuf_tensor(f"junk{s}", [P, D], F32))
                x2o = ea(nc.sbuf_tensor(f"x2o{s}", [P, 2 * D], F32))
                vrow = ea(nc.sbuf_tensor(f"vrow{s}", [1, S], BF16))
                wtt = ea(nc.sbuf_tensor(f"wtt{s}", [P, 3 * FI], BF16,
                                        is_transpose_dest=True)) if s == 0 else None
                crow = ea(nc.sbuf_tensor(f"crow{s}", [1, D], BF16))
                spsum = ea(nc.psum_tensor(f"spsum{s}", [P, S], F32))
                accp = ea(nc.psum_tensor(f"accp{s}", [P, D], F32))
                ptp = ea(nc.psum_tensor(f"ptp{s}", [P, S], BF16))
                blk = ea(nc.Block(f"attn{s}"))
                sch = Sched()

                xtok_ld = sGD.reserve(16)

                def f_xtok(g, s=s):
                    g.dma_start(out=xtok[:].rearrange("p (t d) -> p t d", d=D),
                                in_=x_d[s].rearrange("(t p) d -> p t d", p=P)).then_inc(sGD.sem, 16)
                sch.g(f_xtok)

                a_loaded = sLD.reserve(16)

                def f_aload(sy, w=a_spilled):
                    sy.wait_ge(sGD.sem, w)
                    sy.dma_start(out=A_loc[:], in_=a_scr).then_inc(sLD.sem, 16)
                sch.sy(f_aload)

                xT_all = sLD.reserve(16 * ND)
                xT_done[s] = xT_all

                def f_xT(sy, w=xstage_done[s]):
                    sy.wait_ge(sST.sem, w)
                    for dc in range(ND):
                        sy.dma_start(out=xT[:, dc * S:(dc + 1) * S],
                                     in_=xbf_d[:, dc * P:(dc + 1) * P],
                                     transpose=True).then_inc(sLD.sem, 16)
                sch.sy(f_xT)

                mk_xr = {}
                for i0 in range(2):
                    mk_xr[i0] = sGD.reserve(16)

                def f_xr01(g, s=s):
                    for i0 in range(2):
                        g.dma_start(out=xr[:, i0 * D:(i0 + 1) * D],
                                    in_=x_d[s, i0 * P:(i0 + 1) * P, :]).then_inc(sGD.sem, 16)
                sch.g(f_xr01)

                vr_pe = [sPE.reserve() for _ in range(6)]

                def f_vrow_pe(t, wxt=xT_all, wxtok=xtok_ld):
                    t.wait_ge(sLD.sem, wxt)
                    for vc in range(4):
                        for dc in range(ND):
                            ins = t.matmul(spsum[:1, vc * 512:(vc + 1) * 512],
                                           c2_sb[:, dc:dc + 1],
                                           xT[:, dc * S + vc * 512: dc * S + (vc + 1) * 512],
                                           start=(dc == 0), stop=(dc == ND - 1))
                        ins.then_inc(sPE.sem, 1)
                    for b in range(2):
                        for tc in range(NQT):
                            ins = t.matmul(accp[:1, b * 512:(b + 1) * 512],
                                           onesc_b[:],
                                           xtok[:, tc * D + b * 512: tc * D + (b + 1) * 512],
                                           start=(tc == 0), stop=(tc == NQT - 1))
                        ins.then_inc(sPE.sem, 1)
                sch.t(f_vrow_pe)

                vrow_ev = sV.reserve(6)

                def f_vrow_ev(v, marks=vr_pe):
                    for vc in range(4):
                        v.wait_ge(sPE.sem, marks[vc])
                        v.tensor_scalar(vrow[:, vc * 512:(vc + 1) * 512],
                                        spsum[:1, vc * 512:(vc + 1) * 512],
                                        c0_sb[:, :], None, op0=ALU.add).then_inc(sV.sem, 1)
                    v.wait_ge(sPE.sem, marks[5])
                    for b in range(2):
                        v.tensor_scalar_mul(crow[:, b * 512:(b + 1) * 512],
                                            accp[:1, b * 512:(b + 1) * 512],
                                            -2.0 / S).then_inc(sV.sem, 1)
                sch.v(f_vrow_ev)

                qg_pe = [None] * 16
                qg_ev = [None] * 16
                for it in range(16):
                    ec, sh, half = it // 2, it % 2, it % 2
                    qg_pe[it] = sPE.reserve()

                    def f_qg(t, ec=ec, sh=sh, half=half, it=it,
                             wv=vrow_ev, wa=a_loaded,
                             we=(qg_ev[it - 2] if it >= 2 else None)):
                        if it == 0:
                            t.wait_ge(sV.sem, wv)
                            t.wait_ge(sLD.sem, wa)
                        if we is not None:
                            t.wait_ge(sV.sem, we)
                        for dc in range(ND):
                            for s2 in range(2):
                                ins = t.matmul(
                                    spsum[:, half * D + s2 * 512: half * D + (s2 + 1) * 512],
                                    A_loc[:, dc * D + ec * P: dc * D + (ec + 1) * P],
                                    xT[:, dc * S + sh * D + s2 * 512: dc * S + sh * D + (s2 + 1) * 512],
                                    start=(dc == 0), stop=(dc == ND - 1))
                        ins.then_inc(sPE.sem, 1)
                    sch.t(f_qg)

                    qg_ev[it] = sV.reserve()

                    def f_qev(v, ec=ec, sh=sh, half=half, w=qg_pe[it]):
                        v.wait_ge(sPE.sem, w)
                        v.tensor_copy(qT[:, ec * S + sh * D: ec * S + (sh + 1) * D],
                                      spsum[:, half * D:(half + 1) * D]).then_inc(sV.sem, 1)
                    sch.v(f_qev)
                q_all = qg_ev[15]

                if debug and s == 0:
                    def f_dbgq(g, w=q_all):
                        g.wait_ge(sV.sem, w)
                        g.dma_start(out=dbg["qT"], in_=qT[:]).then_inc(sGD.sem, 16)
                        g.dma_start(out=dbg["xT"], in_=xT[:]).then_inc(sGD.sem, 16)
                        g.dma_start(out=dbg["vrow"], in_=vrow[:]).then_inc(sGD.sem, 16)
                        g.dma_start(out=dbg["crow"], in_=crow[:]).then_inc(sGD.sem, 16)
                    sch.g(f_dbgq)
                    sGD.n += 64

                if s == 0:
                    wjobs = ([("g", dc) for dc in range(ND)]
                             + [("u", dc) for dc in range(ND)]
                             + [("d", ic) for ic in range(NI)])
                    wj_tl = [None] * len(wjobs)
                    wj_cs = [None] * len(wjobs)

                    def emit_wtl(j):
                        wj_tl[j] = sLD.reserve(16)

                        assert j < 3 or wj_cs[j - 3] is not None, f"wtl {j} guard missing"

                        def f_tl(sy, j=j, wcs=(wj_cs[j - 3] if j >= 3 else None),
                                 wsp=wp_spill_done[0]):
                            kind, idx = wjobs[j]
                            if j == 0:
                                sy.wait_ge(sST.sem, wsp)
                            if wcs is not None:
                                sy.wait_ge(sGD.sem, wcs)
                            if kind == "g":
                                inap = wqbf_g[:, idx * P:(idx + 1) * P]
                                wid = FI
                            elif kind == "u":
                                inap = wqbf_u[:, idx * P:(idx + 1) * P]
                                wid = FI
                            else:
                                inap = wqbf_d[:, idx * P:(idx + 1) * P]
                                wid = D
                            sy.dma_start(out=wtt[:, (j % 3) * FI:(j % 3) * FI + wid],
                                         in_=inap, transpose=True).then_inc(sLD.sem, 16)
                        sch.sy(f_tl)

                    def emit_wcs(j):
                        wj_cs[j] = sGD.reserve(16)

                        def f_cs(g, j=j, wtl=wj_tl[j]):
                            kind, idx = wjobs[j]
                            g.wait_ge(sLD.sem, wtl)
                            if kind == "g":
                                out = wgT8_d[:, idx * FI:(idx + 1) * FI]
                                wid = FI
                            elif kind == "u":
                                out = wuT8_d[:, idx * FI:(idx + 1) * FI]
                                wid = FI
                            else:
                                out = wdT8_d[:, idx * D:(idx + 1) * D]
                                wid = D
                            g.dma_start(out=out,
                                        in_=wtt[:, (j % 3) * FI:(j % 3) * FI + wid]).then_inc(sGD.sem, 16)
                        sch.g(f_cs)

                mk_scores = [None] * NQT
                mk_rowmax = [None] * NQT
                mk_exp = [None] * NQT
                mk_recip = [None] * NQT
                mk_pcast = [None] * NQT
                mk_pt = [None] * NQT
                mk_ptev = [None] * NQT
                mk_pax = [None] * NQT
                mk_epi = [None] * NQT
                mk_epi_a = [None] * NQT
                mk_nmean = [None] * NQT
                mk_lnsq = [None] * NQT
                mk_veps = [None] * NQT
                mk_nmr = [None] * NQT
                mk_apply = [None] * NQT

                for i in range(NQT + 2):
                    j = i - 1
                    p_ = i % 2
                    pj = j % 2

                    if s == 0:
                        for jb in range(3 * (i - 1), 3 * i):
                            if 0 <= jb < len(wjobs):
                                emit_wcs(jb)
                        for jb in range(3 * i, 3 * (i + 1)):
                            if jb < len(wjobs):
                                emit_wtl(jb)

                    if 0 <= j < NQT:
                        mk_pt[j] = sPE.reserve()

                        def f_pt(t, j=j, wp=mk_pcast[j],
                                 we=(mk_ptev[j - 1] if j >= 1 else None)):
                            t.wait_ge(sSc.sem, wp)
                            if we is not None:
                                t.wait_ge(sV.sem, we)
                            for tc in range(NQT):
                                ins = t.transpose(ptp[:, tc * P:(tc + 1) * P],
                                                  Pb[:, tc * P:(tc + 1) * P], ident[:])
                            ins.then_inc(sPE.sem, 1)
                        sch.t(f_pt)

                    if i < NQT:
                        mk_scores[i] = sPE.reserve()

                        def f_sc(t, i=i, we=(mk_exp[i - 1] if i >= 1 else None), wq=q_all):
                            if we is not None:
                                t.wait_ge(sSc.sem, we)
                            else:
                                t.wait_ge(sV.sem, wq)
                            for ec in range(ND):
                                for b in range(4):
                                    t.matmul(spsum[:, b * 512:(b + 1) * 512],
                                             qT[:, ec * S + i * P: ec * S + (i + 1) * P],
                                             xT[:, ec * S + b * 512: ec * S + (b + 1) * 512],
                                             start=(ec == 0), stop=False)
                            for b in range(4):
                                ins = t.matmul(spsum[:, b * 512:(b + 1) * 512],
                                               onesr_b[:], vrow[:, b * 512:(b + 1) * 512],
                                               start=False, stop=True)
                            ins.then_inc(sPE.sem, 1)
                        sch.t(f_sc)

                    if 0 <= j < NQT:
                        mk_ptev[j] = sV.reserve()

                        def f_ptev(v, j=j, w=mk_pt[j],
                                   wx=(mk_pax[j - 1] if j >= 1 else None)):
                            v.wait_ge(sPE.sem, w)
                            if wx is not None:
                                v.wait_ge(sPE.sem, wx)
                            v.tensor_copy(PT[:], ptp[:]).then_inc(sV.sem, 1)
                        sch.v(f_ptev)

                    if i < NQT:
                        mk_rowmax[i] = sV.reserve()

                        def f_rm(v, i=i, p_=p_, w=mk_scores[i]):
                            v.wait_ge(sPE.sem, w)
                            v.tensor_reduce(stat(NEGM, p_), spsum[:], axis=AX,
                                            op=ALU.max, negate=True).then_inc(sV.sem, 1)
                        sch.v(f_rm)

                    if i < NQT:
                        mk_exp[i] = sSc.reserve()

                        def f_exp(sce, i=i, p_=p_, w=mk_rowmax[i]):
                            sce.wait_ge(sV.sem, w)
                            sce.activation(Eb[:], spsum[:], AF.Exp,
                                           bias=stat(NEGM, p_), scale=1.0,
                                           accum_out=stat(ZS, p_)).then_inc(sSc.sem, 1)
                        sch.s(f_exp)

                        mk_recip[i] = sV.reserve()

                        def f_rcp(v, i=i, p_=p_, w=mk_exp[i]):
                            v.wait_ge(sSc.sem, w)
                            v.reciprocal(stat(RCPZ, p_), stat(ZS, p_)).then_inc(sV.sem, 1)
                        sch.v(f_rcp)

                        mk_pcast[i] = sSc.reserve()

                        def f_pc(sce, i=i, p_=p_, w=mk_recip[i],
                                 wp=(mk_pt[i - 1] if i >= 1 else None)):
                            sce.wait_ge(sV.sem, w)
                            if wp is not None:
                                sce.wait_ge(sPE.sem, wp)
                            sce.activation(Pb[:], Eb[:], AF.Identity,
                                           scale=stat(RCPZ, p_)).then_inc(sSc.sem, 1)
                        sch.s(f_pc)

                        if debug and s == 0 and i == 0:
                            def f_dbgp(g, w=mk_pcast[0]):
                                g.wait_ge(sSc.sem, w)
                                g.dma_start(out=dbg["E0"], in_=Eb[:]).then_inc(sGD.sem, 16)
                                g.dma_start(out=dbg["P0"], in_=Pb[:]).then_inc(sGD.sem, 16)
                                g.dma_start(out=dbg["st"], in_=st_sb[:]).then_inc(sGD.sem, 16)
                            sch.g(f_dbgp)
                            sGD.n += 48

                    if 0 <= j < NQT:
                        mk_pax[j] = sPE.reserve()

                        def f_pax(t, j=j, w=mk_ptev[j],
                                  wea=(mk_epi_a[j - 1] if j >= 1 else None),
                                  web=(mk_epi[j - 1] if j >= 1 else None)):
                            t.wait_ge(sV.sem, w)
                            for b in range(2):
                                we = wea if b == 0 else web
                                if we is not None:
                                    t.wait_ge(sV.sem, we)
                                t.matmul(accp[:, b * 512:(b + 1) * 512],
                                         onesr_b[:], crow[:, b * 512:(b + 1) * 512],
                                         start=True, stop=False)
                                for tc in range(NQT):
                                    ins = t.matmul(accp[:, b * 512:(b + 1) * 512],
                                                   PT[:, tc * P:(tc + 1) * P],
                                                   xtok[:, tc * D + b * 512: tc * D + (b + 1) * 512],
                                                   start=False, stop=(tc == NQT - 1))
                            ins.then_inc(sPE.sem, 1)
                        sch.t(f_pax)

                        mk_epi_a[j] = sV.reserve()
                        mk_epi[j] = sV.reserve()
                        m_ms = sV.reserve()
                        mk_nmean[j] = sV.reserve()

                        def f_epi(v, j=j, pj=pj, w=mk_pax[j], wxr=mk_xr[j],
                                  wa=(mk_apply[j - 2] if j >= 2 else None),
                                  wn=(mk_nmr[j - 2] if j >= 2 else None),
                                  m1=mk_epi[j], m2=m_ms):
                            v.wait_ge(sPE.sem, w)
                            v.wait_ge(sGD.sem, wxr)
                            if wa is not None:
                                v.wait_ge(sSc.sem, wa)
                            v.scalar_tensor_tensor(ot[:, pj * D:pj * D + 512],
                                                   xr[:, pj * D:pj * D + 512],
                                                   2.0, accp[:, 0:512],
                                                   op0=ALU.mult, op1=ALU.add).then_inc(sV.sem, 1)
                            v.scalar_tensor_tensor(ot[:, pj * D + 512:(pj + 1) * D],
                                                   xr[:, pj * D + 512:(pj + 1) * D],
                                                   2.0, accp[:, 512:1024],
                                                   op0=ALU.mult, op1=ALU.add).then_inc(sV.sem, 1)
                            if wn is not None:
                                v.wait_ge(sG.sem, wn)
                            v.wait_ge(sV.sem, m1)
                            v.tensor_reduce(stat(SUMSQ, pj), ot[:, pj * D:(pj + 1) * D],
                                            axis=AX, op=ALU.add).then_inc(sV.sem, 1)
                            v.wait_ge(sV.sem, m2)
                            v.tensor_scalar_mul(stat(NMEAN, pj), stat(SUMSQ, pj),
                                                -1.0 / D).then_inc(sV.sem, 1)
                        sch.v(f_epi)

                        if i + 1 < NQT:
                            mk_xr[i + 1] = sGD.reserve(16)

                            def f_xrp(g, s=s, i=i, w=mk_epi[j]):
                                g.wait_ge(sV.sem, w)
                                g.dma_start(out=xr[:, ((i + 1) % 2) * D:((i + 1) % 2 + 1) * D],
                                            in_=x_d[s, (i + 1) * P:(i + 2) * P, :]).then_inc(sGD.sem, 16)
                            sch.g(f_xrp)

                    jj = i - 2
                    if 0 <= jj < NQT:
                        pjj = jj % 2

                        mk_lnsq[jj] = sSc.reserve()

                        def f_lnsq(sce, jj=jj, pjj=pjj, w=mk_nmean[jj]):
                            sce.wait_ge(sV.sem, w)
                            sce.activation(junk[:], ot[:, pjj * D:(pjj + 1) * D],
                                           AF.Square, bias=stat(NMEAN, pjj), scale=1.0,
                                           accum_out=stat(SUMSQ, pjj)).then_inc(sSc.sem, 1)
                        sch.s(f_lnsq)

                        m_veps = sG.reserve()
                        m_rstd = sG.reserve()
                        mk_nmr[jj] = sG.reserve()

                        def f_rstd(g, jj=jj, pjj=pjj, w=mk_lnsq[jj],
                                   wa=(mk_apply[jj - 2] if jj >= 2 else None),
                                   m0=m_veps, m1=m_rstd):
                            g.wait_ge(sSc.sem, w)
                            if wa is not None:
                                g.wait_ge(sSc.sem, wa)
                            g.tensor_scalar(stat(VEPS, pjj), stat(SUMSQ, pjj), 1.0 / D, EPS,
                                            op0=ALU.mult, op1=ALU.add).then_inc(sG.sem, 1)
                            g.wait_ge(sG.sem, m0)
                            g.tensor_tensor(stat(RSTD, pjj), stat(VEPS, pjj), neghalf[:],
                                            op=ALU.pow).then_inc(sG.sem, 1)
                            g.wait_ge(sG.sem, m1)
                            g.tensor_tensor(stat(NMR, pjj), stat(NMEAN, pjj), stat(RSTD, pjj),
                                            op=ALU.mult).then_inc(sG.sem, 1)
                        sch.g(f_rstd)

                        mk_apply[jj] = sSc.reserve()

                        def f_apply(sce, jj=jj, pjj=pjj, w=mk_nmr[jj],
                                    wsp=(x2_spill[jj - 2] if jj >= 2 else None)):
                            sce.wait_ge(sG.sem, w)
                            if wsp is not None:
                                sce.wait_ge(sGD.sem, wsp)
                            sce.activation(x2o[:, pjj * D:(pjj + 1) * D],
                                           ot[:, pjj * D:(pjj + 1) * D],
                                           AF.Identity, bias=stat(NMR, pjj),
                                           scale=stat(RSTD, pjj)).then_inc(sSc.sem, 1)
                        sch.s(f_apply)

                        x2_spill[jj] = sGD.reserve(16)
                        if debug and s == 0:
                            sGD.n += 16

                        def f_spill(g, s=s, jj=jj, pjj=pjj, w=mk_apply[jj], dbg_on=(debug and s == 0)):
                            g.wait_ge(sSc.sem, w)
                            g.dma_start(out=x2_d[jj * P:(jj + 1) * P, :],
                                        in_=x2o[:, pjj * D:(pjj + 1) * D]).then_inc(sGD.sem, 16)
                            if dbg_on:
                                g.dma_start(out=dbg["x2"][jj * P:(jj + 1) * P, :],
                                            in_=x2o[:, pjj * D:(pjj + 1) * D]).then_inc(sGD.sem, 16)
                        sch.g(f_spill)

                if s == 0:
                    wp_cast_done[0] = (wj_cs[ND - 1], wj_cs[2 * ND - 1], wj_cs[-1])
                sch.flush(blk)

            with ExitStack() as fx:
                ef = fx.enter_context
                w8g = ef(nc.sbuf_tensor(f"w8g{s}", [P, ND * FI], FP8))
                w8u = ef(nc.sbuf_tensor(f"w8u{s}", [P, ND * FI], FP8))
                w8d = ef(nc.sbuf_tensor(f"w8d{s}", [P, NI * D], FP8))
                x2s = ef(nc.sbuf_tensor(f"x2s{s}", [P, 3 * D], F32))
                tqf = ef(nc.sbuf_tensor(f"tqf{s}", [P, D], F32))
                xq = ef(nc.sbuf_tensor(f"xq{s}", [P, D], BF16))
                xqT = ef(nc.sbuf_tensor(f"xqT{s}", [P, D], BF16))
                rbuf = ef(nc.sbuf_tensor(f"rbuf{s}", [P, 2 * 1024], F32))
                tbuf = ef(nc.sbuf_tensor(f"tbuf{s}", [P, 1024], F32))
                hbuf = ef(nc.sbuf_tensor(f"hbuf{s}", [P, FI], F32))
                thq = ef(nc.sbuf_tensor(f"thq{s}", [P, 2 * 1024], F32))
                hqb = ef(nc.sbuf_tensor(f"hqb{s}", [P, FI], BF16))
                hqT = ef(nc.sbuf_tensor(f"hqT{s}", [P, FI], BF16))
                hst = ef(nc.sbuf_tensor(f"hst{s}", [P, 4], F32))
                ot2 = ef(nc.sbuf_tensor(f"ot2{s}", [P, 2 * D], F32))
                junk2 = ef(nc.sbuf_tensor(f"junk2{s}", [P, D], F32))
                yb = ef(nc.sbuf_tensor(f"yb{s}", [P, 2 * D], F32))
                gp = ef(nc.psum_tensor(f"gp{s}", [P, 1024], F32))
                up = ef(nc.psum_tensor(f"up{s}", [P, 1024], F32))
                dp = ef(nc.psum_tensor(f"dp{s}", [P, 1024], F32))
                trb = ef(nc.psum_tensor(f"trb{s}", [P, 1024], BF16))
                trb2 = ef(nc.psum_tensor(f"trb2{s}", [P, 1024], BF16))
                blk = ef(nc.Block(f"ffn{s}"))
                sch = Sched()
                tpu = []

                prep_done = wp_cast_done[0] if s == 0 else (None, None, None)
                wgu_ready = sLD.reserve(48)
                wd_ready = wgu_ready

                def f_reload(sy, s=s, pg=prep_done[0] if s == 0 else None,
                             pd=prep_done[2] if s == 0 else None):
                    if pg is not None:
                        sy.wait_ge(sGD.sem, pg)
                    sy.dma_start(out=w8g[:], in_=wgT8_d).then_inc(sLD.sem, 16)
                    if pd is not None:
                        sy.wait_ge(sGD.sem, pd)
                    sy.dma_start(out=w8d[:], in_=wdT8_d).then_inc(sLD.sem, 16)
                sch.sy(f_reload)

                def f_reload2(sce, s=s, pu=prep_done[1] if s == 0 else None):
                    if pu is not None:
                        sce.wait_ge(sGD.sem, pu)
                    sce.dma_start(out=w8u[:], in_=wuT8_d).then_inc(sLD.sem, 16)
                sch.s(f_reload2)

                x2_mk = {}
                for jj in range(2):
                    x2_mk[jj] = sGD.reserve(16)

                    def f_x2p(g, jj=jj, w=x2_spill[jj]):
                        g.wait_ge(sGD.sem, w)
                        g.dma_start(out=x2s[:, (jj % 3) * D:(jj % 3 + 1) * D],
                                    in_=x2_d[jj * P:(jj + 1) * P, :]).then_inc(sGD.sem, 16)
                    sch.g(f_x2p)

                mk_sa = [None] * NQT
                mk_tq = [None] * NQT
                mk_xq = [None] * NQT
                mk_xqt = [None] * NQT
                mk_xqte = [None] * NQT
                mk_gate = [[None] * 4 for _ in range(NQT)]
                mk_up = [[None] * 4 for _ in range(NQT)]
                mk_r = [[None] * 4 for _ in range(NQT)]
                mk_t1 = [[None] * 4 for _ in range(NQT)]
                mk_h = [[None] * 4 for _ in range(NQT)]
                mk_seff = [None] * NQT
                mk_thq = [None] * NQT
                mk_hq = [None] * NQT
                mk_hqt = [[None] * 4 for _ in range(NQT)]
                mk_hqte = [[None] * 4 for _ in range(NQT)]
                mk_down = [None] * NQT
                mk_ydt = [None] * NQT
                mk_lnsq2 = [None] * NQT
                mk_nmr2 = [None] * NQT
                mk_apply2 = [None] * NQT
                mk_yst = [None] * NQT
                mk_nmean2 = [None] * NQT
                mk_veps2 = [None] * NQT

                def sa_ops(jj):
                    pjj = jj % 2
                    return [
                        lambda v: v.tensor_reduce(stat(AMAX, pjj), x2s[:, (jj % 3) * D:(jj % 3 + 1) * D],
                                                  axis=AX, op=ALU.max, apply_absolute_value=True),
                        lambda v: v.tensor_scalar_max(stat(C1, pjj), stat(AMAX, pjj), EPS),
                        lambda v: v.reciprocal(stat(SA, pjj), stat(C1, pjj)),
                        lambda v: v.tensor_scalar_mul(stat(SA, pjj), stat(SA, pjj), 127.0),
                        lambda v: v.tensor_scalar_mul(stat(T0, pjj), stat(C1, pjj), 1.0 / 127.0),
                        lambda v: v.tensor_tensor(stat(T1, pjj), stat(T0, pjj), stat(T0, pjj), op=ALU.mult),
                        lambda v: v.tensor_tensor(stat(CGHU, pjj), stat(T1, pjj), stat(T0, pjj), op=ALU.mult),
                        lambda v: v.tensor_tensor(stat(CGHU, pjj), stat(CGHU, pjj), bc_sb[:, 0:1], op=ALU.mult),
                    ]

                def em_sa(jn):
                    mk_sa[jn] = chain(sch.v, sV, sa_ops(jn), pre=[(sGD.sem, x2_mk[jn])])

                def em_tq(jn):
                    mk_tq[jn] = sSc.reserve()

                    def f_tq(sce, jn=jn, wsa=mk_sa[jn],
                             wxq=(mk_xq[jn - 1] if jn >= 1 else None)):
                        sce.wait_ge(sV.sem, wsa)
                        if wxq is not None:
                            sce.wait_ge(sV.sem, wxq)
                        sce.activation(tqf[:], x2s[:, (jn % 3) * D:(jn % 3 + 1) * D],
                                       AF.Identity, bias=crne_b[:],
                                       scale=stat(SA, jn % 2)).then_inc(sSc.sem, 1)
                    sch.s(f_tq)

                def em_sa_tq(jn):
                    em_sa(jn)
                    em_tq(jn)

                def em_xq(jn):
                    mk_xq[jn] = sV.reserve()

                    def f_xq(v, jn=jn, w=mk_tq[jn],
                             wt=(mk_xqt[jn - 1] if jn >= 1 else None)):
                        v.wait_ge(sSc.sem, w)
                        if wt is not None:
                            v.wait_ge(sPE.sem, wt)
                        v.tensor_scalar_add(xq[:], tqf[:], -CRNE).then_inc(sV.sem, 1)
                    sch.v(f_xq)

                def em_xqt(jn):
                    mk_xqt[jn] = sPE.reserve()
                    tb = trb if len(tpu) % 2 == 0 else trb2
                    tguard = tpu[-2] if len(tpu) >= 2 else None

                    def f_xqt(t, jn=jn, w=mk_xq[jn], tb=tb, tguard=tguard):
                        t.wait_ge(sV.sem, w)
                        if tguard is not None:
                            t.wait_ge(tguard[0].sem, tguard[1])
                        for dc in range(ND):
                            ins = t.transpose(tb[:, dc * P:(dc + 1) * P],
                                              xq[:, dc * P:(dc + 1) * P], ident[:])
                        ins.then_inc(sPE.sem, 1)
                    sch.t(f_xqt)
                    mk_xqte[jn] = sSc.reserve()

                    def f_xqte(sce, jn=jn, w=mk_xqt[jn], tb=tb,
                               wp=(mk_up[jn - 1][3] if jn >= 1 else None)):
                        sce.wait_ge(sPE.sem, w)
                        if wp is not None:
                            sce.wait_ge(sPE.sem, wp)
                        sce.activation(xqT[:], tb[:], AF.Identity).then_inc(sSc.sem, 1)
                    sch.s(f_xqte)
                    tpu.append((sSc, mk_xqte[jn]))

                def em_guq(jn, q):
                    mk_gate[jn][q] = sPE.reserve()
                    mk_up[jn][q] = sPE.reserve()
                    prev_jq = None if (jn == 0 and q == 0) else ((jn, q - 1) if q > 0 else (jn - 1, 3))

                    def f_gu(t, jn=jn, q=q, wx=mk_xqte[jn],
                             wr=(mk_r[prev_jq[0]][prev_jq[1]] if prev_jq else None),
                             wt1=(mk_t1[prev_jq[0]][prev_jq[1]] if prev_jq else None),
                             wld=(wgu_ready if (jn == 0 and q == 0) else None)):
                        t.wait_ge(sSc.sem, wx)
                        if wld is not None:
                            t.wait_ge(sLD.sem, wld)
                        if wr is not None:
                            t.wait_ge(sSc.sem, wr)
                        for ih in range(2):
                            for dc in range(ND):
                                ins = t.matmul(
                                    gp[:, ih * 512:(ih + 1) * 512],
                                    xqT[:, dc * P:(dc + 1) * P],
                                    w8g[:, dc * FI + q * 1024 + ih * 512: dc * FI + q * 1024 + (ih + 1) * 512],
                                    start=(dc == 0), stop=(dc == ND - 1))
                        ins.then_inc(sPE.sem, 1)
                        if wt1 is not None:
                            t.wait_ge(sV.sem, wt1)
                        for ih in range(2):
                            for dc in range(ND):
                                ins = t.matmul(
                                    up[:, ih * 512:(ih + 1) * 512],
                                    xqT[:, dc * P:(dc + 1) * P],
                                    w8u[:, dc * FI + q * 1024 + ih * 512: dc * FI + q * 1024 + (ih + 1) * 512],
                                    start=(dc == 0), stop=(dc == ND - 1))
                        ins.then_inc(sPE.sem, 1)
                    sch.t(f_gu)

                def em_relu(jn, q):
                    mk_r[jn][q] = sSc.reserve()
                    hprev = (jn, q - 2) if q >= 2 else (jn - 1, q + 2)

                    def f_relu(sce, jn=jn, q=q, wg=mk_gate[jn][q],
                               wh=(mk_h[hprev[0]][hprev[1]] if hprev[0] >= 0 else None)):
                        sce.wait_ge(sPE.sem, wg)
                        if wh is not None:
                            sce.wait_ge(sV.sem, wh)
                        sce.activation(rbuf[:, (q % 2) * 1024:(q % 2 + 1) * 1024],
                                       gp[:], AF.Relu).then_inc(sSc.sem, 1)
                    sch.s(f_relu)

                def em_hh(jn, q, with_stat=True):
                    mk_t1[jn][q] = sV.reserve()
                    m_h1 = sV.reserve()
                    mk_h[jn][q] = sV.reserve() if with_stat else m_h1

                    def f_h(v, jn=jn, q=q, wr=mk_r[jn][q], wu=mk_up[jn][q],
                            wthq=(mk_thq[jn - 1] if (jn >= 1 and q == 0) else None),
                            m1=mk_t1[jn][q], m2=m_h1, with_stat=with_stat):
                        v.wait_ge(sSc.sem, wr)
                        v.wait_ge(sPE.sem, wu)
                        v.tensor_tensor(tbuf[:], up[:],
                                        rbuf[:, (q % 2) * 1024:(q % 2 + 1) * 1024],
                                        op=ALU.mult).then_inc(sV.sem, 1)
                        if wthq is not None:
                            v.wait_ge(sSc.sem, wthq)
                        v.wait_ge(sV.sem, m1)
                        v.tensor_tensor(hbuf[:, q * 1024:(q + 1) * 1024], tbuf[:],
                                        rbuf[:, (q % 2) * 1024:(q % 2 + 1) * 1024],
                                        op=ALU.mult).then_inc(sV.sem, 1)
                        if with_stat:
                            v.wait_ge(sV.sem, m2)
                            v.tensor_reduce(hst[:, q:q + 1],
                                            hbuf[:, q * 1024:(q + 1) * 1024],
                                            axis=AX, op=ALU.max,
                                            apply_absolute_value=True).then_inc(sV.sem, 1)
                    sch.v(f_h)

                em_sa_tq(0)
                em_xq(0)
                em_xqt(0)
                em_guq(0, 0)
                em_relu(0, 0)
                em_hh(0, 0)

                if debug and s == 0:
                    def f_dbg1(g, w=mk_xq[0], w2=wd_ready):
                        g.wait_ge(sV.sem, w)
                        g.dma_start(out=dbg["xq0"], in_=xq[:]).then_inc(sGD.sem, 16)
                        g.wait_ge(sLD.sem, w2)
                        g.dma_start(out=dbg["w8g"], in_=w8g[:, 0:FI]).then_inc(sGD.sem, 16)
                        g.dma_start(out=dbg["w8d"], in_=w8d[:, 0:D]).then_inc(sGD.sem, 16)
                    sch.g(f_dbg1)
                    sGD.n += 48

                for j in range(NQT):
                    pj = j % 2

                    if j + 1 < NQT:
                        em_sa(j + 1)

                    for q in (1, 2, 3):
                        em_guq(j, q)
                        em_relu(j, q)
                        em_hh(j, q, with_stat=(q != 3))
                        if q == 1 and j + 1 < NQT:
                            em_tq(j + 1)
                        if q == 2 and j + 1 < NQT:
                            em_xq(j + 1)

                    if j + 1 < NQT:
                        em_xqt(j + 1)

                    mk_seff[j] = chain(sch.v, sV, [
                        (lambda v, j=j: v.tensor_reduce(hst[:, 3:4], hbuf[:, 3 * 1024:4 * 1024],
                                                        axis=AX, op=ALU.max, apply_absolute_value=True)),
                        (lambda v, pj=pj: v.tensor_reduce(stat(HMAX, pj), hst[:, 0:4], axis=AX, op=ALU.max)),
                        (lambda v, pj=pj: v.tensor_scalar(stat(BCLIP, pj), stat(HMAX, pj),
                                                          stat(CGHU, pj), EPS,
                                                          op0=ALU.mult, op1=ALU.max)),
                        (lambda v, pj=pj: v.reciprocal(stat(SEFF, pj), stat(BCLIP, pj))),
                        (lambda v, pj=pj: v.tensor_scalar(stat(SEFF, pj), stat(SEFF, pj),
                                                          stat(CGHU, pj), 127.0,
                                                          op0=ALU.mult, op1=ALU.mult)),
                        (lambda v, pj=pj: v.tensor_tensor(stat(CDOWN, pj), stat(BCLIP, pj), bc_sb[:, 1:2], op=ALU.mult)),
                    ], pre=[(sV.sem, mk_h[j][3])])

                    thq_mks = [sSc.reserve() for _ in range(4)]
                    hq_mks = [sV.reserve() for _ in range(4)]

                    def f_thq(sce, j=j, pj=pj, wseff=mk_seff[j], hm=hq_mks,
                              whqprev=(mk_hq[j - 1] if j >= 1 else None)):
                        sce.wait_ge(sV.sem, wseff)
                        if whqprev is not None:
                            sce.wait_ge(sV.sem, whqprev)
                        for cq in range(4):
                            if cq >= 2:
                                sce.wait_ge(sV.sem, hm[cq - 2])
                            sce.activation(thq[:, (cq % 2) * 1024:(cq % 2 + 1) * 1024],
                                           hbuf[:, cq * 1024:(cq + 1) * 1024],
                                           AF.Identity, bias=crne_b[:],
                                           scale=stat(SEFF, pj)).then_inc(sSc.sem, 1)
                    sch.s(f_thq)

                    def f_hq(v, j=j, tm=thq_mks, wtr=(mk_hqt[j - 1][3] if j >= 1 else None)):
                        if wtr is not None:
                            v.wait_ge(sPE.sem, wtr)
                        for cq in range(4):
                            v.wait_ge(sSc.sem, tm[cq])
                            v.tensor_scalar_add(hqb[:, cq * 1024:(cq + 1) * 1024],
                                                thq[:, (cq % 2) * 1024:(cq % 2 + 1) * 1024],
                                                -CRNE).then_inc(sV.sem, 1)
                    sch.v(f_hq)
                    mk_thq[j] = thq_mks[3]
                    mk_hq[j] = hq_mks[3]

                    if debug and s == 0 and j == 0:
                        def f_dbg2(g, w=mk_hq[0]):
                            g.wait_ge(sV.sem, w)
                            g.dma_start(out=dbg["hq0"], in_=hqb[:]).then_inc(sGD.sem, 16)
                            g.dma_start(out=dbg["h0"], in_=hbuf[:]).then_inc(sGD.sem, 16)
                        sch.g(f_dbg2)
                        sGD.n += 32

                    if j + 1 < NQT:
                        em_guq(j + 1, 0)

                    for grp in range(4):
                        mk_hqt[j][grp] = sPE.reserve()
                        tb = trb if len(tpu) % 2 == 0 else trb2
                        tguard = tpu[-2] if len(tpu) >= 2 else None

                        def f_hqt(t, j=j, grp=grp, whq=hq_mks[grp], tb=tb, tguard=tguard):
                            t.wait_ge(sV.sem, whq)
                            if tguard is not None:
                                t.wait_ge(tguard[0].sem, tguard[1])
                            for ic in range(8):
                                icc = grp * 8 + ic
                                ins = t.transpose(tb[:, ic * P:(ic + 1) * P],
                                                  hqb[:, icc * P:(icc + 1) * P], ident[:])
                            ins.then_inc(sPE.sem, 1)
                        sch.t(f_hqt)

                        if grp % 2 == 0:
                            mk_hqte[j][grp] = sSc.reserve()

                            def f_hqte(sce, j=j, grp=grp, w=mk_hqt[j][grp], tb=tb,
                                       wd=(mk_down[j - 1] if (j >= 1 and grp == 0) else None)):
                                sce.wait_ge(sPE.sem, w)
                                if wd is not None:
                                    sce.wait_ge(sPE.sem, wd)
                                sce.activation(hqT[:, grp * 1024:(grp + 1) * 1024],
                                               tb[:], AF.Identity).then_inc(sSc.sem, 1)
                            sch.s(f_hqte)
                            tpu.append((sSc, mk_hqte[j][grp]))
                        else:
                            mk_hqte[j][grp] = sV.reserve()

                            def f_hqte(v, j=j, grp=grp, w=mk_hqt[j][grp], tb=tb):
                                v.wait_ge(sPE.sem, w)
                                v.tensor_copy(hqT[:, grp * 1024:(grp + 1) * 1024],
                                              tb[:]).then_inc(sV.sem, 1)
                            sch.v(f_hqte)
                            tpu.append((sV, mk_hqte[j][grp]))

                    if j + 1 < NQT:
                        em_relu(j + 1, 0)
                        em_hh(j + 1, 0)

                    mk_down[j] = sPE.reserve()

                    def f_down(t, j=j, we=mk_hqte[j][3], we2=mk_hqte[j][2],
                               we1=mk_hqte[j][1],
                               wy=(mk_ydt[j - 1] if j >= 1 else None),
                               wld=(wd_ready if j == 0 else None)):
                        t.wait_ge(sV.sem, we1)
                        t.wait_ge(sSc.sem, we2)
                        if wld is not None:
                            t.wait_ge(sLD.sem, wld)
                        if wy is not None:
                            t.wait_ge(sV.sem, wy)
                        for b in range(2):
                            for ic in range(24):
                                t.matmul(dp[:, b * 512:(b + 1) * 512],
                                         hqT[:, ic * P:(ic + 1) * P],
                                         w8d[:, ic * D + b * 512: ic * D + (b + 1) * 512],
                                         start=(ic == 0), stop=False)
                        t.wait_ge(sV.sem, we)
                        for b in range(2):
                            for ic in range(24, NI):
                                ins = t.matmul(dp[:, b * 512:(b + 1) * 512],
                                               hqT[:, ic * P:(ic + 1) * P],
                                               w8d[:, ic * D + b * 512: ic * D + (b + 1) * 512],
                                               start=False, stop=(ic == NI - 1))
                        ins.then_inc(sPE.sem, 1)
                    sch.t(f_down)

                    mk_ydt[j] = sV.reserve()
                    m_ms = sV.reserve()
                    mk_nmean2[j] = sV.reserve()

                    def f_ydt(v, j=j, pj=pj, wd=mk_down[j],
                              wa=(mk_apply2[j - 2] if j >= 2 else None),
                              wn=(mk_nmr2[j - 2] if j >= 2 else None),
                              m1=mk_ydt[j], m2=m_ms):
                        v.wait_ge(sPE.sem, wd)
                        if wa is not None:
                            v.wait_ge(sSc.sem, wa)
                        v.scalar_tensor_tensor(ot2[:, pj * D:(pj + 1) * D], dp[:],
                                               stat(CDOWN, pj), x2s[:, (j % 3) * D:(j % 3 + 1) * D],
                                               op0=ALU.mult, op1=ALU.add).then_inc(sV.sem, 1)
                        if wn is not None:
                            v.wait_ge(sG.sem, wn)
                        v.wait_ge(sV.sem, m1)
                        v.tensor_reduce(stat(SUMSQ, pj), ot2[:, pj * D:(pj + 1) * D],
                                        axis=AX, op=ALU.add).then_inc(sV.sem, 1)
                        v.wait_ge(sV.sem, m2)
                        v.tensor_scalar_mul(stat(NMEAN, pj), stat(SUMSQ, pj),
                                            -1.0 / D).then_inc(sV.sem, 1)
                    sch.v(f_ydt)

                    if j + 2 < NQT:
                        x2_mk[j + 2] = sGD.reserve(16)

                        def f_x2n(g, j=j, wy=(mk_ydt[j - 1] if j >= 1 else None), wsp=x2_spill[j + 2]):
                            if wy is not None:
                                g.wait_ge(sV.sem, wy)
                            g.wait_ge(sGD.sem, wsp)
                            g.dma_start(out=x2s[:, ((j + 2) % 3) * D:((j + 2) % 3 + 1) * D],
                                        in_=x2_d[(j + 2) * P:(j + 3) * P, :]).then_inc(sGD.sem, 16)
                        sch.g(f_x2n)

                    for jj in ([j - 1] if j >= 1 else []) + ([NQT - 1] if j == NQT - 1 else []):
                        pjj = jj % 2

                        mk_lnsq2[jj] = sSc.reserve()

                        def f_lnsq(sce, jj=jj, pjj=pjj, w=mk_nmean2[jj]):
                            sce.wait_ge(sV.sem, w)
                            sce.activation(junk2[:], ot2[:, pjj * D:(pjj + 1) * D], AF.Square,
                                           bias=stat(NMEAN, pjj), scale=1.0,
                                           accum_out=stat(SUMSQ, pjj)).then_inc(sSc.sem, 1)
                        sch.s(f_lnsq)

                        m_veps = sG.reserve()
                        m_rstd = sG.reserve()
                        mk_nmr2[jj] = sG.reserve()

                        def f_rstd(g, jj=jj, pjj=pjj, w=mk_lnsq2[jj],
                                   wa=(mk_apply2[jj - 2] if jj >= 2 else None),
                                   m0=m_veps, m1=m_rstd):
                            g.wait_ge(sSc.sem, w)
                            if wa is not None:
                                g.wait_ge(sSc.sem, wa)
                            g.tensor_scalar(stat(VEPS, pjj), stat(SUMSQ, pjj), 1.0 / D, EPS,
                                            op0=ALU.mult, op1=ALU.add).then_inc(sG.sem, 1)
                            g.wait_ge(sG.sem, m0)
                            g.tensor_tensor(stat(RSTD, pjj), stat(VEPS, pjj), neghalf[:],
                                            op=ALU.pow).then_inc(sG.sem, 1)
                            g.wait_ge(sG.sem, m1)
                            g.tensor_tensor(stat(NMR, pjj), stat(NMEAN, pjj), stat(RSTD, pjj),
                                            op=ALU.mult).then_inc(sG.sem, 1)
                        sch.g(f_rstd)

                        mk_apply2[jj] = sSc.reserve()

                        def f_apply(sce, jj=jj, pjj=pjj, w=mk_nmr2[jj],
                                    wy=(mk_yst[jj - 2] if jj >= 2 else None)):
                            sce.wait_ge(sG.sem, w)
                            if wy is not None:
                                sce.wait_ge(sGD.sem, wy)
                            sce.activation(yb[:, pjj * D:(pjj + 1) * D],
                                           ot2[:, pjj * D:(pjj + 1) * D],
                                           AF.Identity, bias=stat(NMR, pjj),
                                           scale=stat(RSTD, pjj)).then_inc(sSc.sem, 1)
                        sch.s(f_apply)

                        mk_yst[jj] = sGD.reserve(16)

                        def f_yst(g, jj=jj, pjj=pjj, w=mk_apply2[jj]):
                            g.wait_ge(sSc.sem, w)
                            g.dma_start(out=y_d[s, jj * P:(jj + 1) * P, :],
                                        in_=yb[:, pjj * D:(pjj + 1) * D]).then_inc(sGD.sem, 16)
                        sch.g(f_yst)

                if s == 0:
                    emit_xstage(sch, 1)
                sch.flush(blk)

        with nc.Block("fin") as blk:
            @blk.gpsimd
            def _(g):
                g.wait_ge(sGD.sem, sGD.n)

            @blk.sync
            def _(sy):
                sy.wait_ge(sST.sem, sST.n)

    return nc


_NC_CACHE = {}


def kernel(L0_attn_x, L0_attn_Wq, L0_attn_bq, L0_attn_Wk, L0_attn_bk,
           L0_ffn_w_gate, L0_ffn_w_up, L0_ffn_w_down):
    x = np.ascontiguousarray(np.asarray(L0_attn_x, dtype=np.float32))
    Wq = np.ascontiguousarray(np.asarray(L0_attn_Wq, dtype=np.float32))
    bq = np.ascontiguousarray(np.asarray(L0_attn_bq, dtype=np.float32))
    Wk = np.ascontiguousarray(np.asarray(L0_attn_Wk, dtype=np.float32))
    bk = np.ascontiguousarray(np.asarray(L0_attn_bk, dtype=np.float32))
    Wg = np.ascontiguousarray(np.asarray(L0_ffn_w_gate, dtype=np.float32))
    Wu = np.ascontiguousarray(np.asarray(L0_ffn_w_up, dtype=np.float32))
    Wd = np.ascontiguousarray(np.asarray(L0_ffn_w_down, dtype=np.float32))

    if "nc" not in _NC_CACHE:
        _NC_CACHE["nc"] = build_program()
    nc = _NC_CACHE["nc"]

    in_maps = []
    for c in range(NCORES):
        in_maps.append({
            "x": x[NSEQ * c: NSEQ * (c + 1)],
            "Wq": Wq, "bq": bq, "Wk": Wk, "bk": bk,
            "Wg": Wg, "Wu": Wu, "Wd": Wd,
        })
    res = run_bass_kernel_spmd(nc, in_maps, core_ids=list(range(NCORES)))
    out = np.concatenate([res.results[c]["y"] for c in range(NCORES)], axis=0)
    return out
